# version 03
# baseline (speedup 1.0000x reference)
import numpy as np
import ml_dtypes
from contextlib import ExitStack

import concourse.bass as bass
import concourse.mybir as mybir
from concourse.bass_utils import run_bass_kernel_spmd

F32 = mybir.dt.float32
BF16 = mybir.dt.bfloat16
I32 = mybir.dt.int32
AF = mybir.ActivationFunctionType
ALU = mybir.AluOpType
AX = mybir.AxisListType.X

D = 1024
H = 16
HD = 64
INNER = 4096
NCD = D // 128
NCI = INNER // 128
EPS = 1e-5

TTOT = 4096
PHASE = 99
DBG = False
CHUNK = 512
SCRD_OFF = 0

_PH_ORD = {1: 10, 2: 20, 15: 30, 21: 40, 22: 50, 224: 60, 225: 70,
           23: 80, 3: 90, 4: 100}


def PHLE(x):
    return _PH_ORD.get(PHASE, 10 ** 9) <= _PH_ORD[x]
SLAB = 256
GRP = 8

ENG_SEM = {"tensor": "sT", "vector": "sV", "scalar": "sS", "gpsimd": "sG", "sync": "sQ"}


DMA_SEMS = {"q": ("dq0", "dq1"), "g": ("dg0", "dg1")}


class Prog:
    def __init__(self):
        self.ops = {e: [] for e in ("sync", "tensor", "vector", "scalar", "gpsimd")}
        self.counts = {s: 0 for s in ("sT", "sV", "sS", "sG", "sQ",
                                      "dq0", "dq1", "dg0", "dg1")}
        self.dma_rr = {"q": 0, "g": 0}

    def op(self, engine, emit, deps=(), sig=False, dma=None):
        incs = []
        ev = None
        if dma is not None:
            pool = DMA_SEMS[dma]
            s = pool[self.dma_rr[dma] % len(pool)]
            self.dma_rr[dma] += 1
            deps = list(deps) + [(s, self.counts[s])]
            self.counts[s] += 16
            incs.append((s, 16))
            ev = (s, self.counts[s])
        elif sig:
            s = ENG_SEM[engine]
            self.counts[s] += 1
            incs.append((s, 1))
            ev = (s, self.counts[s])
        dd = {}
        for d in deps:
            if d is not None:
                dd[d[0]] = max(dd.get(d[0], 0), d[1])
        self.ops[engine].append((emit, sorted(dd.items()), incs))
        return ev


def build_nc(ttot=TTOT, chunk=CHUNK, slab=SLAB):
    nch = ttot // chunk
    nslab = chunk // slab
    ngrp = slab // GRP

    nc = bass.Bass("TRN2", target_bir_lowering=False)
    p = Prog()

    dr = {}

    def din(name, shape, dt=F32):
        dr[name] = nc.dram_tensor(name, list(shape), dt, kind="ExternalInput").ap()
        return dr[name]

    din("x_fm", [D, ttot])
    din("qT", [D, D]); din("kT", [D, D]); din("vT", [D, D]); din("oT", [D, D])
    din("fcB", [NCI * D, 128]); din("gB", [NCI * D, 128]); din("uB", [NCI * D, 128])
    din("projB", [NCD * INNER, 128]); din("dB", [NCD * INNER, 128])
    din("wmod", [D, 6 * D])
    din("cfm", [128, NCD])
    din("qb_r", [1, D]); din("kb_r", [1, D]); din("vb_r", [1, D])
    din("ob_f", [128, NCD]); din("projb_f", [128, NCD]); din("fcb_f", [128, NCI])
    din("ln1w_f", [128, NCD]); din("ln1b_f", [128, NCD])
    din("ln2w_f", [128, NCD]); din("ln2b_f", [128, NCD])
    din("mask128", [128, 128], BF16)
    din("ident64", [64, 64], BF16)
    din("ident128f", [128, 128], F32)
    out_fm = nc.dram_tensor("out_fm", [D, ttot], F32, kind="ExternalOutput").ap()
    dbg = {}
    if DBG:
        for nm, shp, dt in (("dbg_h", [128, NCD * chunk], F32),
                            ("dbg_xn", [128, NCD * chunk], BF16),
                            ("dbg_kq", [64, H * slab], BF16),
                            ("dbg_kk", [64, H * slab], BF16),
                            ("dbg_kv", [64, H * slab], BF16),
                            ("dbg_amid", [128, NCD * chunk], BF16),
                            ("dbg_afm", [128, NCD * chunk], F32),
                            ("dbg_gfm", [128, NCI * chunk], BF16),
                            ("dbg_x1", [128, NCD * chunk], F32),
                            ("dbg_h2", [128, NCD * chunk], F32),
                            ("dbg_xq", [128, NCD * chunk], BF16),
                            ("dbg_pq", [128, NCI * chunk], BF16),
                            ("dbg_mods", [128, 48], F32),
                            ("dbg_sbd", [128, 128], BF16),
                            ("dbg_vst", [128, 64], BF16),
                            ("dbg_recd", [128, 32], F32),
                            ("dbg_recb", [128, 64], F32),
                            ("dbg_sp", [128, 128], F32),
                            ("dbg_np", [128, 64], F32),
                            ("dbg_dump", [128, 2048], BF16),
                            ("dbg_mask", [128, 128], BF16)):
            dbg[nm] = nc.dram_tensor(nm, shp, dt, kind="ExternalOutput").ap()

    scr = {}
    for nm, shape in (("q", [D, D]), ("k", [D, D]), ("v", [D, D]), ("o", [D, D]),
                      ("fc", [NCI * D, 128]), ("g", [NCI * D, 128]), ("u", [NCI * D, 128]),
                      ("proj", [NCD * INNER, 128]), ("d", [NCD * INNER, 128])):
        scr[nm] = nc.dram_tensor("scr_" + nm, shape, BF16, kind="Internal").ap()

    ctx = ExitStack()
    sb = lambda name, shape, dt=F32: ctx.enter_context(nc.sbuf_tensor(name, list(shape), dt))
    psal = lambda name, shape, dt=F32: ctx.enter_context(nc.psum_tensor(name, list(shape), dt))

    xfm = sb("xfm", [128, NCD * chunk])
    hfm = sb("hfm", [128, NCD * chunk])
    afm = sb("afm", [128, NCD * chunk])
    mmin = sb("mmin", [128, NCD * chunk], BF16)
    khq = sb("khq", [64, H * slab], BF16)
    khk = sb("khk", [64, H * slab], BF16)
    khv = sb("khv", [64, H * slab], BF16)
    amid = sb("amid", [128, NCD * chunk], BF16)
    xbf = amid
    gfm = sb("gfm", [128, NCI * chunk], BF16)
    x2bf = sb("x2bf", [128, NCD * chunk], BF16)
    wbig = [sb("wbigA", [128, 8 * D], BF16), sb("wbigB", [128, 8 * D], BF16)]

    if PHASE == 231:
        recb = [sb("recb%d" % i, [128, 64]) for i in range(3)]
        sbd = [sb("sbd%d" % i, [128, 128], BF16) for i in range(3)]
        vstr = [sb("vst%d" % i, [128, 64], BF16) for i in range(3)]
    else:
        sbd = [sb("sbd%d" % i, [128, 128], BF16) for i in range(3)]
        recb = [sb("recb%d" % i, [128, 64]) for i in range(3)]
        vstr = [sb("vst%d" % i, [128, 64], BF16) for i in range(3)]
    recd = sb("recd", [128, ngrp])
    rbjunk = sb("rbjunk", [128, 64]) if PHASE == 228 else None
    gtr = [sb("gt%d" % i, [128, chunk]) for i in range(2)]
    sigr = [sb("sig%d" % i, [128, chunk], BF16) for i in range(2)]
    gur = [sb("gu%d" % i, [128, chunk]) for i in range(2)]
    Er = [gtr[i][:, :].bitcast(BF16)[0:64, 0:4 * slab] for i in range(2)]
    EBr = [gur[i][:, :].bitcast(BF16)[0:64, 0:4 * slab] for i in range(2)]
    vtmp = [sb("vtmp%d" % i, [128, chunk]) for i in range(2)]
    qi = vtmp[0][:, :].bitcast(I32)
    trA = sb("trA", [128, chunk])
    trAb = x2bf[:, 0:chunk]
    trBb = x2bf[:, chunk:2 * chunk]
    maxcol = sb("maxcol", [128, 4])
    rsxg_t = sb("rsxg_t", [128, chunk])
    sxb2_t = sb("sxb2_t", [128, chunk])
    scaleb_t = sb("scaleb_t", [128, chunk])
    rows = [sb("row%d" % i, [1, chunk]) for i in range(7)]
    srow = sb("srow", [1, 8])
    swb = sb("swb", [128, 4])
    cb = {}
    for nm, shp, dt in (("ob_f", [128, NCD], F32), ("projb_f", [128, NCD], F32),
                        ("fcb_f", [128, NCI], F32),
                        ("ln1w_f", [128, NCD], F32), ("ln1b_f", [128, NCD], F32),
                        ("ln2w_f", [128, NCD], F32), ("ln2b_f", [128, NCD], F32),
                        ("mask128", [128, 128], BF16), ("ident64", [64, 64], BF16),
                        ("ident128f", [128, 128], F32), ("cfm", [128, NCD], F32)):
        cb[nm] = sb("c_" + nm, shp, dt)
    qbb = sb("qbb", [1, D], BF16)
    kbb = sb("kbb", [1, D], BF16)
    vbb = sb("vbb", [1, D], BF16)
    onescol_b = sb("onescol_b", [128, 1], BF16)
    onesrow_f = sb("onesrow_f", [1, 128])
    onesrow_b = sb("onesrow_b", [1, chunk], BF16)
    onecol_f = sb("onecol_f", [128, 1])
    ceps = sb("ceps", [1, 1])
    cl127 = sb("cl127", [1, 1])
    mods = sb("mods", [128, 48])
    siluc = sb("siluc", [128, NCD])
    m8 = sb("m8", [128, 8])
    prep_b = gfm[:, 0:2048]
    prep_i = gfm[:, 4096:8192].bitcast(I32)
    prefA = gfm[:, 8192:12288].bitcast(F32)
    prefB = gfm[:, 12288:16384].bitcast(F32)
    prep_r = sb("prep_r", [128, 18])

    pP = [psal("pA", [128, 1024]), psal("pB", [128, 1024])]
    bcp = [psal("bcA", [128, 512]), psal("bcB", [128, 512])]
    midp = [psal("midA", [128, 512]), psal("midB", [128, 512])]
    vst_ps = [midp[0][:, 264:296].bitcast(BF16), midp[1][:, 264:296].bitcast(BF16)]

    sems = {s: ctx.enter_context(nc.semaphore(s)) for s in
            ("sT", "sV", "sS", "sG", "sQ", "dq0", "dq1", "dg0", "dg1")}

    BT = {"bc0": [], "bc1": [], "mid0": [], "mid1": []}

    def bank_w(name):
        d = BT[name]
        BT[name] = []
        return d

    def bank_r(name, ev):
        if ev is not None:
            BT[name].append(ev)

    for nm in cb:
        p.op("sync", (lambda nm: lambda s: s.dma_start(out=cb[nm][:], in_=dr[nm]))(nm),
             dma="q")
    ev_const = [("dq0", p.counts["dq0"]), ("dq1", p.counts["dq1"])]

    if DBG and PHASE in (232, 99):
        ev_sent = p.op("vector", lambda v: v.memset(mmin[:, 0:128], 7.0), sig=True)
        p.op("sync", lambda s: s.dma_start(out=dbg["dbg_sbd"], in_=mmin[:, 0:128]),
             deps=[ev_sent], dma="q")
    ev_ones = p.op("vector", lambda v: v.memset(onescol_b[:], 1.0), sig=True)
    ev_onesrf = p.op("vector", lambda v: v.memset(onesrow_f[:], 1.0), sig=True)
    ev_onesrb = p.op("vector", lambda v: v.memset(onesrow_b[:], 1.0), sig=True)
    ev_ones2 = p.op("vector", lambda v: v.memset(onecol_f[:], 1.0), sig=True)
    p.op("vector", lambda v: v.memset(ceps[:], EPS))
    ev_cc = p.op("vector", lambda v: v.memset(cl127[:], float(np.log(127.0))), sig=True)
    ev_bb = None
    for nm_, dst in (("qb_r", qbb), ("kb_r", kbb), ("vb_r", vbb)):
        ev_bb = p.op("gpsimd", (lambda nm_, dst: lambda g: g.dma_start(
            out=dst[:], in_=dr[nm_]))(nm_, dst), dma="g")

    ev_silu = p.op("scalar", lambda s: s.activation(out=siluc[:], in_=cb["cfm"][:],
                                                    func=AF.Silu),
                   deps=ev_const, sig=True)
    ev_mod_mm = None
    wmod_hist = []
    for kc in range(8):
        for part in range(3):
            buf = prefA if (kc * 3 + part) % 2 == 0 else prefB
            evd = p.op("sync", (lambda kc, part, buf: lambda s: s.dma_start(
                out=buf[:], in_=dr["wmod"][128 * kc:128 * (kc + 1),
                                           2048 * part:2048 * (part + 1)]))(kc, part, buf),
                deps=(wmod_hist[-2] if len(wmod_hist) >= 2 else None,), dma="q")
            last = None
            for m in range(16):
                mg = part * 16 + m
                sig = (m == 15)
                last = p.op("tensor", (lambda m, mg, kc, buf: lambda t: t.matmul(
                    out=bcp[0][:, mg:mg + 1], lhsT=buf[:, 128 * m:128 * (m + 1)],
                    rhs=siluc[:, kc:kc + 1],
                    start=(kc == 0 and mg == 0), stop=(kc == 7 and mg == 47)))(m, mg, kc, buf),
                    deps=[evd, ev_silu] + (bank_w("bc0") if (kc == 0 and mg == 0) else []),
                    sig=sig)
                if kc == 7 and mg == 47:
                    ev_mod_mm = last
            wmod_hist.append(last)
    if ev_mod_mm is None:
        ev_mod_mm = wmod_hist[-1]
    ev_mods = p.op("vector", lambda v: v.tensor_copy(out=mods[:], in_=bcp[0][:, 0:48]),
                   deps=(ev_mod_mm,), sig=True)
    bank_r("bc0", ev_mods)
    ev_ca1 = p.op("vector", lambda v: v.tensor_scalar(out=mods[:, 8:16], in0=mods[:, 8:16],
                                                      scalar1=1.0, scalar2=None, op0=ALU.add),
                  deps=(ev_mods,), sig=True)
    ev_cb1 = p.op("vector", lambda v: v.tensor_scalar(out=mods[:, 32:40], in0=mods[:, 32:40],
                                                      scalar1=1.0, scalar2=None, op0=ALU.add),
                  deps=(ev_mods,), sig=True)
    ev_g = ev_mods
    for col in (16, 40):
        ev_s0 = p.op("scalar", (lambda col: lambda s: s.activation(
            out=m8[:], in_=mods[:, col:col + 8], func=AF.Sign))(col),
            deps=(ev_g,), sig=True)
        ev_sq = p.op("vector", lambda v: v.scalar_tensor_tensor(
            out=trA[:, 0:8], in0=m8[:], scalar=1.0, in1=m8[:],
            op0=ALU.mult, op1=ALU.mult), deps=(ev_s0,), sig=True)
        ev_sq2 = p.op("vector", lambda v: v.tensor_scalar(
            out=trA[:, 0:8], in0=trA[:, 0:8], scalar1=-1.0, scalar2=1.0,
            op0=ALU.mult, op1=ALU.add), deps=(ev_sq,), sig=True)
        ev_sgn = p.op("vector", lambda v: v.tensor_tensor(
            out=m8[:], in0=m8[:], in1=trA[:, 0:8], op=ALU.add),
            deps=(ev_sq2,), sig=True)
        ev_a1 = p.op("vector", (lambda col: lambda v: v.scalar_tensor_tensor(
            out=trA[:, 0:8], in0=mods[:, col:col + 8], scalar=-1.0,
            in1=mods[:, col:col + 8], op0=ALU.mult, op1=ALU.max))(col),
            deps=(ev_g, ev_sgn), sig=True)
        ev_a2 = p.op("vector", lambda v: v.tensor_scalar(
            out=trA[:, 0:8], in0=trA[:, 0:8], scalar1=0.25, scalar2=None,
            op0=ALU.max), deps=(ev_a1,), sig=True)
        ev_g = p.op("vector", (lambda col: lambda v: v.tensor_tensor(
            out=mods[:, col:col + 8], in0=m8[:], in1=trA[:, 0:8],
            op=ALU.mult))(col), deps=(ev_a2, ev_sgn), sig=True)
    ev_mods_done = p.op("vector", lambda v: v.tensor_copy(out=m8[:], in_=mods[:, 0:8]),
                        deps=(ev_g, ev_ca1, ev_cb1), sig=True)

    prep_done = {}

    def prep_cast(src, dst, nrow128, ncols, first_dep=None):
        evo = None
        evi = first_dep
        for r in range(nrow128):
            for c0 in range(0, ncols, 2048):
                cw = min(2048, ncols - c0)
                evi = p.op("gpsimd", (lambda r, c0, cw: lambda g: g.dma_start(
                    out=prep_b[:, 0:cw], in_=src[128 * r:128 * (r + 1), c0:c0 + cw]))(r, c0, cw),
                    deps=(evo, evi), dma="g")
                evo = p.op("sync", (lambda r, c0, cw: lambda s: s.dma_start(
                    out=dst[128 * r:128 * (r + 1), c0:c0 + cw], in_=prep_b[:, 0:cw]))(r, c0, cw),
                    deps=(evi,), dma="q")
        return evo

    def prep_quant(src, dst, nrow128, ncols, sw_col, first_dep=None):
        nsl = nrow128 * (ncols // 2048)
        ev_acc = None
        evd_hist = []
        red_hist = []
        i = 0
        for r in range(nrow128):
            for c0 in range(0, ncols, 2048):
                buf = prefA if i % 2 == 0 else prefB
                evd = p.op("gpsimd", (lambda r, c0, buf: lambda g: g.dma_start(
                    out=buf[:], in_=src[128 * r:128 * (r + 1), c0:c0 + 2048]))(r, c0, buf),
                    deps=([red_hist[-2]] if len(red_hist) >= 2 else list(first_dep or [])),
                    dma="g")
                ev_red = p.op("vector", (lambda i, buf: lambda v: v.tensor_reduce(
                    out=prep_r[:, i % 16:i % 16 + 1], in_=buf[:], axis=AX, op=ALU.add,
                    apply_absolute_value=True))(i, buf), deps=(evd,), sig=True)
                red_hist.append(ev_red)
                i += 1
                if i % 16 == 0 or i == nsl:
                    ncl = 16 if i % 16 == 0 else i % 16
                    first = i <= 16
                    tgt = 0 if first else 2
                    ev_f = p.op("vector", (lambda ncl, tgt: lambda v: v.tensor_scalar(
                        out=prep_r[:, 16:16 + ncl] if ncl <= 2 else prep_r[:, 0:ncl],
                        in0=prep_r[:, 0:ncl], scalar1=1.0, scalar2=None,
                        op0=ALU.mult, op1=ALU.add,
                        accum_out=trA[:, tgt:tgt + 1]))(ncl, tgt),
                        deps=(ev_red, ev_acc), sig=True)
                    if not first:
                        ev_f = p.op("vector", lambda v: v.tensor_tensor(
                            out=trA[:, 0:1], in0=trA[:, 0:1], in1=trA[:, 2:3], op=ALU.add),
                            deps=(ev_f,), sig=True)
                    ev_acc = ev_f
        ev_mm = p.op("tensor", lambda t: t.matmul(out=bcp[1][0:1, 0:1], lhsT=trA[:, 0:1],
                                                  rhs=onecol_f[:], start=True, stop=True),
                     deps=[ev_acc, ev_ones2] + bank_w("bc1"), sig=True)
        n = float(nrow128 * 128 * ncols)
        ev_r1 = p.op("vector", lambda v: v.tensor_scalar(
            out=rows[5][:, 0:1], in0=bcp[1][0:1, 0:1], scalar1=1.0 / n, scalar2=1e-5,
            op0=ALU.mult, op1=ALU.max), deps=(ev_mm,), sig=True)
        ev_is = p.op("vector", (lambda sw_col: lambda v: v.tensor_copy(
            out=srow[:, sw_col:sw_col + 1], in_=rows[5][:, 0:1]))(sw_col),
            deps=(ev_r1,), sig=True)
        ev_r2 = p.op("vector", lambda v: v.reciprocal(out=rows[5][:, 1:2], in_=rows[5][:, 0:1]),
                     deps=(ev_is,), sig=True)
        bank_r("bc1", ev_r1)
        ev_bc = p.op("tensor", lambda t: t.matmul(
            out=bcp[1][:, 4:5], lhsT=onesrow_f[:], rhs=rows[5][0:1, 1:2],
            start=True, stop=True), deps=[ev_r2, ev_onesrf] + bank_w("bc1"), sig=True)
        ev_swb = p.op("vector", (lambda sw_col: lambda v: v.tensor_copy(
            out=swb[:, sw_col:sw_col + 1], in_=bcp[1][:, 4:5]))(sw_col),
            deps=(ev_bc,), sig=True)
        bank_r("bc1", ev_swb)
        evo = None
        evq_hist = []
        i = 0
        for r in range(nrow128):
            for c0 in range(0, ncols, 2048):
                buf = prefA if i % 2 == 0 else prefB
                evd = p.op("gpsimd", (lambda r, c0, buf: lambda g: g.dma_start(
                    out=buf[:], in_=src[128 * r:128 * (r + 1), c0:c0 + 2048]))(r, c0, buf),
                    deps=(evq_hist[-2] if len(evq_hist) >= 2 else ev_swb,), dma="g")
                evi = p.op("vector", (lambda buf, sw_col: lambda v: v.tensor_scalar(
                    out=prep_i[:], in0=buf[:], scalar1=swb[:, sw_col:sw_col + 1],
                    scalar2=None, op0=ALU.mult))(buf, sw_col),
                    deps=(evd, ev_swb, evo), sig=True)
                evc = p.op("vector", lambda v: v.tensor_scalar(
                    out=prep_b[:], in0=prep_i[:], scalar1=1.0, scalar2=-1.0,
                    op0=ALU.min, op1=ALU.max), deps=(evi,), sig=True)
                evo = p.op("sync", (lambda r, c0: lambda s: s.dma_start(
                    out=dst[128 * r:128 * (r + 1), c0:c0 + 2048], in_=prep_b[:]))(r, c0),
                    deps=(evc,), dma="q")
                evq_hist.append(evo)
                i += 1
        return evo

    fl = lambda apx: apx.rearrange("(r c) m -> r (c m)", c=16)
    prep_done["q"] = prep_cast(dr["qT"], scr["q"], NCD, D)
    prep_done["k"] = prep_cast(dr["kT"], scr["k"], NCD, D, first_dep=prep_done["q"])
    prep_done["v"] = prep_cast(dr["vT"], scr["v"], NCD, D, first_dep=prep_done["k"])
    prep_done["o"] = prep_cast(dr["oT"], scr["o"], NCD, D, first_dep=prep_done["v"])
    prep_done["fc"] = prep_cast(fl(dr["fcB"]), fl(scr["fc"]), NCI * D // 16 // 128, 2048,
                                first_dep=prep_done["o"])
    prep_done["proj"] = prep_cast(fl(dr["projB"]), fl(scr["proj"]),
                                  NCD * INNER // 16 // 128, 2048,
                                  first_dep=prep_done["fc"])
    prep_done["g"] = prep_quant(fl(dr["gB"]), fl(scr["g"]), NCI * D // 16 // 128, 2048, 0,
                                first_dep=[ev_mod_mm, prep_done["proj"]])
    prep_done["u"] = prep_quant(fl(dr["uB"]), fl(scr["u"]), NCI * D // 16 // 128, 2048, 1,
                                first_dep=[prep_done["g"]])
    prep_done["d"] = prep_quant(fl(dr["dB"]), fl(scr["d"]), NCD * INNER // 16 // 128, 2048, 2,
                                first_dep=[prep_done["u"]])

    wring = {"i": 0, "free": [None, None]}

    def wload(src_slice, ncols, kcn, extra_dep=None):
        i = wring["i"] % 2
        wring["i"] += 1
        ev = p.op("sync", (lambda i, src_slice, ncols, kcn: lambda s: s.dma_start(
            out=wbig[i][:, 0:ncols].rearrange("p (kc m) -> p kc m", kc=kcn),
            in_=src_slice))(i, src_slice, ncols, kcn),
            deps=(wring["free"][i], extra_dep), dma="q")
        return wbig[i], ev, i

    def wfree(slot, ev):
        wring["free"][slot] = ev

    x_src = dr["x_fm"].rearrange("(c p) t -> p c t", p=128)
    out_dst = out_fm.rearrange("(c p) t -> p c t", p=128)

    mmfree = [None, None, None, None]

    def pslot(i):
        return pP[i // 2][:, (i % 2) * 512:(i % 2) * 512 + 512]

    def emit_ln(src, dst, wcol_scale, wcol_bias, lnw, lnb, src_evs, extra_dst_deps=()):
        dep_all = list(src_evs)
        evc = p.op("vector", lambda v: v.tensor_copy(out=xbf[:], in_=src[:]),
                   deps=dep_all, sig=True)
        evs = p.op("vector", lambda v: v.scalar_tensor_tensor(
            out=x2bf[:], in0=src[:], scalar=1.0, in1=src[:], op0=ALU.mult, op1=ALU.mult),
            deps=dep_all, sig=True)
        ev_mm = None
        for c in range(NCD):
            ev_mm = p.op("tensor", (lambda c: lambda t: t.matmul(
                out=bcp[0][0:1, 0:chunk], lhsT=onescol_b[:],
                rhs=xbf[:, c * chunk:(c + 1) * chunk],
                start=(c == 0), stop=(c == NCD - 1)))(c),
                deps=[evc, ev_ones] + (bank_w("bc0") if c == 0 else []), sig=False)
        for c in range(NCD):
            ev_mm = p.op("tensor", (lambda c: lambda t: t.matmul(
                out=bcp[0][32:33, 0:chunk], lhsT=onescol_b[:],
                rhs=x2bf[:, c * chunk:(c + 1) * chunk],
                start=(c == 0), stop=(c == NCD - 1)))(c),
                deps=(evs,), sig=(c == NCD - 1))
        ev_mu = p.op("vector", lambda v: v.tensor_scalar(
            out=rows[0][:], in0=bcp[0][0:1, 0:chunk], scalar1=1.0 / D, scalar2=None,
            op0=ALU.mult), deps=(ev_mm,), sig=True)
        ev_m2 = p.op("vector", lambda v: v.tensor_tensor(
            out=rows[3][:], in0=rows[0][:], in1=rows[0][:], op=ALU.mult),
            deps=(ev_mu,), sig=True)
        ev_var = p.op("vector", lambda v: v.scalar_tensor_tensor(
            out=rows[4][:], in0=bcp[0][32:33, 0:chunk], scalar=1.0 / D,
            in1=rows[3][:], op0=ALU.mult, op1=ALU.subtract),
            deps=(ev_m2,), sig=True)
        bank_r("bc0", ev_var)
        ev_ln = p.op("scalar", lambda s: s.activation(
            out=rows[3][:], in_=rows[4][:], func=AF.Ln, bias=ceps[:]),
            deps=(ev_var, ev_cc), sig=True)
        ev_istd = p.op("scalar", lambda s: s.activation(
            out=rows[1][:], in_=rows[3][:], func=AF.Exp, scale=-0.5),
            deps=(ev_ln,), sig=True)
        ev_R = p.op("vector", lambda v: v.tensor_tensor(
            out=rows[2][:], in0=rows[0][:], in1=rows[1][:], op=ALU.mult),
            deps=(ev_istd,), sig=True)
        ev_bA = p.op("tensor", lambda t: t.matmul(
            out=bcp[0][:, 0:chunk], lhsT=onesrow_f[:], rhs=rows[1][:],
            start=True, stop=True), deps=[ev_istd, ev_mu, ev_onesrf] + bank_w("bc0"),
            sig=True)
        ev_bB = p.op("tensor", lambda t: t.matmul(
            out=bcp[1][:, 0:chunk], lhsT=onesrow_f[:], rhs=rows[2][:],
            start=True, stop=True), deps=[ev_R] + bank_w("bc1"), sig=True)
        evs_out = []
        for c in range(NCD):
            if lnw is None:
                scl = mods[:, wcol_scale + c:wcol_scale + c + 1]
                bia = mods[:, wcol_bias + c:wcol_bias + c + 1]
            else:
                scl = lnw[:, c:c + 1]
                bia = lnb[:, c:c + 1]
            vt = vtmp[c % 2]
            dep1 = [ev_bA, ev_mods_done] + dep_all
            if len(evs_out) >= 2:
                dep1.append(evs_out[-2])
            ev1 = p.op("vector", (lambda c, scl, vt: lambda v: v.scalar_tensor_tensor(
                out=vt[:], in0=src[:, c * chunk:(c + 1) * chunk], scalar=scl,
                in1=bcp[0][:, 0:chunk], op0=ALU.mult, op1=ALU.mult))(c, scl, vt),
                deps=dep1, sig=True)
            bank_r("bc0", ev1)
            ev2 = p.op("vector", (lambda c, scl, vt: lambda v: v.scalar_tensor_tensor(
                out=vt[:], in0=bcp[1][:, 0:chunk], scalar=scl, in1=vt[:],
                op0=ALU.mult, op1=ALU.subtract))(c, scl, vt),
                deps=(ev1, ev_bB), sig=True)
            bank_r("bc1", ev2)
            ev3 = p.op("scalar", (lambda c, bia, vt: lambda s: s.activation(
                out=dst[:, c * chunk:(c + 1) * chunk], in_=vt[:], func=AF.Identity,
                bias=bia, scale=-1.0))(c, bia, vt),
                deps=[ev2] + list(extra_dst_deps), sig=True)
            evs_out.append(ev3)
        return evs_out

    def part_max(src_ev):
        ev_t = None
        bw0 = bank_w("mid0")
        bw1 = bank_w("mid1")
        for b in range(4):
            ev_t = p.op("tensor", (lambda b: lambda t: t.transpose(
                out=midp[b % 2][:, 128 * (b // 2):128 * (b // 2) + 128],
                in_=trA[:, 128 * b:128 * (b + 1)], identity=cb["ident128f"][:]))(b),
                deps=[src_ev] + ev_const + (bw0 if b == 0 else (bw1 if b == 1 else [])),
                sig=(b >= 2))
        evs_r = []
        for b in range(4):
            evs_r.append(p.op("vector", (lambda b: lambda v: v.tensor_reduce(
                out=maxcol[:, b:b + 1], in_=midp[b % 2][:, 128 * (b // 2):128 * (b // 2) + 128],
                axis=AX, op=ALU.max))(b),
                deps=(ev_t,), sig=True))
        bank_r("mid0", evs_r[0]); bank_r("mid0", evs_r[2])
        bank_r("mid1", evs_r[1]); bank_r("mid1", evs_r[3])
        ev_f = None
        bwf = bank_w("bc1")
        for b in range(4):
            ev_f = p.op("tensor", (lambda b: lambda t: t.matmul(
                out=bcp[1][32:33, 128 * b:128 * (b + 1)], lhsT=maxcol[:, b:b + 1],
                rhs=cb["ident128f"][:], start=(b == 0), stop=(b == 3)))(b),
                deps=[evs_r[-1]] + (bwf if b == 0 else []), sig=(b == 3))
        ev_mr = p.op("vector", lambda v: v.tensor_scalar(
            out=rows[6][:], in0=bcp[1][32:33, 0:chunk], scalar1=1e-5, scalar2=None,
            op0=ALU.max), deps=(ev_f,), sig=True)
        bank_r("bc1", ev_mr)
        return ev_mr

    ev_store_prev = None
    ev_x_prev_readers = []
    ringfree = [None, None, None]

    for ci in range(nch):
        ev_x = p.op("sync", (lambda ci: lambda s: s.dma_start(
            out=xfm[:], in_=x_src[:, :, ci * chunk:(ci + 1) * chunk]))(ci),
            deps=[ev_store_prev] + ev_x_prev_readers, dma="q")
        ev_x_prev_readers = []

        evs_h = emit_ln(xfm, hfm, 8, 0, None, None,
                        [ev_x] + ([ev_mods_done] if ci == 0 else []))
        evs_xn = emit_ln(hfm, mmin, 0, 0, cb["ln1w_f"], cb["ln1b_f"], evs_h)
        if DBG and ci == 0:
            p.op("sync", lambda s: s.dma_start(out=dbg["dbg_mask"], in_=cb["mask128"][:]),
                 deps=ev_const, dma="q")
            p.op("sync", lambda s: s.dma_start(out=dbg["dbg_h"], in_=hfm[:]),
                 deps=evs_h + [("sV", p.counts["sV"]), ("sS", p.counts["sS"])], dma="q")
            p.op("sync", lambda s: s.dma_start(out=dbg["dbg_xn"], in_=mmin[:]),
                 deps=evs_xn, dma="q")
            p.op("sync", lambda s: s.dma_start(out=dbg["dbg_mods"], in_=mods[:]),
                 deps=[ev_mods_done], dma="q")

        if PHLE(1):
            ev_store_prev = p.op("sync", (lambda ci: lambda s: s.dma_start(
                out=out_dst[:, :, ci * chunk:(ci + 1) * chunk], in_=xfm[:]))(ci),
                deps=evs_xn, dma="q")
            ev_x_prev_readers = list(evs_xn)
            continue

        ev_amid_last = None
        ev_qkv_last = None
        for si in range(nslab):
            t0 = si * slab
            kh = {"q": khq, "k": khk, "v": khv}
            bias_b = {"q": qbb, "k": kbb, "v": vbb}
            ev_kh = {}
            for TN in ("q", "k", "v"):
                wsrc = scr[TN].rearrange("(kc pp) m -> pp kc m", pp=128)
                wb, evw, slot = wload(wsrc, 8 * D, 8, extra_dep=prep_done[TN])
                ev_evacs = []
                for hb in range(4):
                    bank = pP[hb % 2]
                    ev_mm = None
                    for hs in range(4):
                        h = hb * 4 + hs
                        for kc in range(NCD):
                            st = (hs % 2 == 0 and kc == 0)
                            ev_mm = p.op("tensor",
                                         (lambda h, hs, kc, wb, bank, t0, st: lambda t: t.matmul(
                                             out=bank[0:64, hs * slab:(hs + 1) * slab],
                                             lhsT=wb[:, kc * D + h * 64:kc * D + (h + 1) * 64],
                                             rhs=mmin[:, kc * chunk + t0:kc * chunk + t0 + slab],
                                             start=st, stop=False))(h, hs, kc, wb, bank, t0, st),
                                         deps=[evw, evs_xn[kc],
                                               mmfree[2 * (hb % 2)], mmfree[2 * (hb % 2) + 1]],
                                         sig=False)
                        ev_mm = p.op("tensor",
                                     (lambda h, hs, bank, TN: lambda t: t.matmul(
                                         out=bank[0:64, hs * slab:(hs + 1) * slab],
                                         lhsT=bias_b[TN][0:1, h * 64:(h + 1) * 64],
                                         rhs=onesrow_b[0:1, 0:slab],
                                         start=False, stop=(hs % 2 == 1)))(h, hs, bank, TN),
                                     deps=(ev_bb, ev_onesrb), sig=(hs == 3))
                    pin = bank[0:64, 0:4 * slab]
                    pin_v = pin.rearrange("p (hs t) -> p t hs", hs=4)
                    khdst = kh[TN][:, :].rearrange("p (t hh) -> p t hh", hh=H)[
                        :, :, 4 * hb:4 * hb + 4]
                    if TN == "v":
                        ev_e = p.op("vector",
                                    (lambda pin_v, khdst: lambda v: v.tensor_copy(
                                        out=khdst, in_=pin_v))(pin_v, khdst),
                                    deps=(ev_mm,), sig=True)
                    else:
                        er = Er[hb % 2]
                        ebr = EBr[hb % 2]
                        ev_a = p.op("scalar",
                                    (lambda pin, er: lambda s: s.activation(
                                        out=er, in_=pin, func=AF.Exp))(pin, er),
                                    deps=(ev_mm,), sig=True)
                        ev_b = p.op("vector",
                                    (lambda er, ebr: lambda v: v.tensor_scalar(
                                        out=ebr, in0=er,
                                        scalar1=1.0, scalar2=None, op0=ALU.min))(er, ebr),
                                    deps=(ev_a,), sig=True)
                        ebr_v = ebr.rearrange("p (hs t) -> p t hs", hs=4)
                        ev_e = p.op("vector",
                                    (lambda pin_v, ebr_v, khdst: lambda v: v.scalar_tensor_tensor(
                                        out=khdst, in0=pin_v, scalar=0.0, in1=ebr_v,
                                        op0=ALU.max, op1=ALU.add))(pin_v, ebr_v, khdst),
                                    deps=(ev_b,), sig=True)
                    ev_evacs.append(ev_e)
                    mmfree[2 * (hb % 2)] = ev_e
                    mmfree[2 * (hb % 2) + 1] = ev_e
                ev_kh[TN] = ev_evacs[-1]
                wfree(slot, ev_evacs[-1])
            dep_kh = [ev_kh["q"], ev_kh["k"], ev_kh["v"]]
            if DBG and ci == 0 and si == 0:
                p.op("sync", lambda s: s.dma_start(out=dbg["dbg_kq"], in_=khq[:]),
                     deps=dep_kh, dma="q")
                p.op("sync", lambda s: s.dma_start(out=dbg["dbg_kk"], in_=khk[:]),
                     deps=dep_kh, dma="q")
                p.op("sync", lambda s: s.dma_start(out=dbg["dbg_kv"], in_=khv[:]),
                     deps=dep_kh, dma="q")
            ev_qkv_last = ev_kh["v"]

            if PHLE(15):
                ev_amid_last = None
                continue

            grp_done = [None] * ngrp
            for g in range(ngrp):
                mname = "mid%d" % (g % 2)
                mb = midp[g % 2]
                vb = vst_ps[g % 2]
                sbg = sbd[g % 3]
                vt = vstr[g % 3]
                rb = recb[g % 3]
                ring_free = ringfree[g % 3]
                bw = bank_w(mname)
                ev_vt = p.op("tensor", (lambda g, vb: lambda t: t.transpose(
                    out=vb, in_=khv[:, 128 * g:128 * (g + 1)],
                    identity=cb["ident64"][:]))(g, vb),
                    deps=[*ev_const, ev_kh["v"]] + bw, sig=True)
                ev_vste = p.op("vector", (lambda vb, vt: lambda v: v.tensor_copy(
                    out=vt[:], in_=vb))(vb, vt), deps=(ev_vt, ring_free), sig=True)
                bank_r(mname, ev_vste)
                ev_S = p.op("tensor", (lambda g, mb: lambda t: t.matmul(
                    out=mb[:, 0:128], lhsT=khk[:, 128 * g:128 * (g + 1)],
                    rhs=khq[:, 128 * g:128 * (g + 1)], start=True, stop=True))(g, mb),
                    deps=dep_kh + [ev_vste], sig=True)
                ev_e = []
                for par in range(2):
                    s_in = mb[:, 0:128].rearrange("p (j hh pr) -> p hh j pr",
                                                  j=8, hh=8, pr=2)[:, :, :, par]
                    m_in = cb["mask128"][:, :].rearrange("p (j hh pr) -> p hh j pr",
                                                         j=8, hh=8, pr=2)[:, :, :, par]
                    s_out = sbg[:, 64 * par:64 * (par + 1)].rearrange(
                        "p (hh j) -> p hh j", hh=8)
                    ev_e.append(p.op("vector", (lambda s_in, m_in, s_out: lambda v:
                                     v.scalar_tensor_tensor(
                                         out=s_out, in0=s_in, scalar=1.0, in1=m_in,
                                         op0=ALU.mult, op1=ALU.mult))(s_in, m_in, s_out),
                                     deps=[ev_S, ring_free] + ev_const, sig=True))
                    bank_r(mname, ev_e[-1])
                if PHLE(21):
                    if DBG and ci == 0 and si == 0 and g == 0:
                        e1 = p.op("sync", (lambda sbg: lambda s: s.dma_start(
                            out=dbg["dbg_sbd"], in_=sbg[:]))(sbg), deps=ev_e, dma="q")
                        e2 = p.op("sync", (lambda vt: lambda s: s.dma_start(
                            out=dbg["dbg_vst"], in_=vt[:]))(vt), deps=[ev_vste], dma="q")
                        bank_r(mname, e1); bank_r(mname, e2)
                        ringfree[0] = e2
                    grp_done[g] = ev_e[-1] if PHASE == 21 else ev_vste
                    if not (DBG and ci == 0 and si == 0 and g == 0):
                        ringfree[g % 3] = grp_done[g]
                    continue
                ev_tapS = None
                if DBG and ci == 0 and si == 0 and g == 0:
                    ev_tapS = p.op("vector", (lambda mb: lambda v: v.tensor_copy(
                        out=trA[:, 0:128], in_=mb[:, 0:128]))(mb), deps=ev_e, sig=True)
                    p.op("sync", lambda s: s.dma_start(out=dbg["dbg_sp"], in_=trA[:, 0:128]),
                         deps=[ev_tapS], dma="q")
                ev_den = p.op("tensor", (lambda mb, sbg: lambda t: t.matmul(
                    out=mb[:, 256:257], lhsT=sbg[:], rhs=onescol_b[:],
                    start=True, stop=True))(mb, sbg),
                    deps=ev_e + [ev_ones, ev_tapS], sig=True)
                ev_rc = p.op("vector", (lambda g, mb: lambda v: v.reciprocal(
                    out=recd[:, g:g + 1], in_=mb[:, 256:257]))(g, mb),
                    deps=(ev_den,), sig=True)
                bank_r(mname, ev_rc)
                if PHLE(22):
                    if DBG and ci == 0 and si == 0 and g == 0:
                        e1 = p.op("sync", (lambda sbg: lambda s: s.dma_start(
                            out=dbg["dbg_sbd"], in_=sbg[:]))(sbg), deps=ev_e + [ev_rc], dma="q")
                        e3 = p.op("sync", lambda s: s.dma_start(
                            out=dbg["dbg_recd"], in_=recd[:]), deps=[ev_rc], dma="q")
                        bank_r(mname, e1); bank_r(mname, e3)
                        ringfree[0] = e3
                        grp_done[g] = e3
                    else:
                        grp_done[g] = ev_rc
                        ringfree[g % 3] = ev_rc
                    continue
                ev_n0 = p.op("tensor", (lambda mb, sbg, vt: lambda t: t.matmul(
                    out=mb[0:64, 128:192], lhsT=vt[:], rhs=sbg[:, 0:64],
                    start=True, stop=True))(mb, sbg, vt),
                    deps=[ev_vste, ev_rc] + ev_e, sig=(PHASE == 224))
                if PHLE(224):
                    grp_done[g] = ev_n0 if PHASE == 224 else ev_rc
                    ringfree[g % 3] = grp_done[g]
                    continue
                ev_n1 = p.op("tensor", (lambda mb, sbg, vt: lambda t: t.matmul(
                    out=mb[64:128, 128:192], lhsT=vt[:], rhs=sbg[:, 64:128],
                    start=True, stop=True))(mb, sbg, vt), deps=(), sig=(PHASE == 225))
                if PHLE(225):
                    if DBG and ci == 0 and si == 0 and g == 0:
                        e1 = p.op("sync", (lambda sbg: lambda s: s.dma_start(
                            out=dbg["dbg_sbd"], in_=sbg[:]))(sbg),
                            deps=ev_e + [ev_rc, ev_n1], dma="q")
                        ev_tapN = p.op("vector", (lambda mb: lambda v: v.tensor_copy(
                            out=trA[:, 128:192], in_=mb[:, 128:192]))(mb),
                            deps=[ev_n1, ev_rc], sig=True)
                        e2 = p.op("sync", lambda s: s.dma_start(
                            out=dbg["dbg_np"], in_=trA[:, 128:192]),
                            deps=[ev_tapN], dma="q")
                        bank_r(mname, e1); bank_r(mname, ev_tapN)
                        ringfree[0] = e2
                        grp_done[g] = e2
                    else:
                        grp_done[g] = ev_n1 if PHASE == 225 else ev_rc
                        ringfree[g % 3] = grp_done[g]
                    continue
                if PHASE == 226 and g > 0:
                    grp_done[g] = ev_n1
                    ringfree[g % 3] = ev_n1
                    continue
                ev_rb0 = p.op("tensor", (lambda g, mb: lambda t: t.matmul(
                    out=mb[0:64, 192:256],
                    lhsT=recd[:, g:g + 1].broadcast_to((128, 64)),
                    rhs=cb["ident128f"][:, 0:64], start=True, stop=True))(g, mb),
                    deps=(ev_rc,), sig=False)
                ev_rb1 = p.op("tensor", (lambda g, mb: lambda t: t.matmul(
                    out=mb[64:128, 192:256],
                    lhsT=recd[:, g:g + 1].broadcast_to((128, 64)),
                    rhs=cb["ident128f"][:, 64:128], start=True, stop=True))(g, mb),
                    deps=(), sig=True)
                if PHASE == 227 and g > 0:
                    grp_done[g] = ev_rb1
                    ringfree[g % 3] = ev_rb1
                    continue
                if PHASE == 228 and g > 0:
                    ev_rbe = p.op("vector", (lambda mb: lambda v: v.tensor_copy(
                        out=rbjunk[:], in_=mb[:, 192:256]))(mb), deps=(ev_rb1,), sig=True)
                elif PHASE == 229 and g > 0:
                    ev_rbe = p.op("vector", (lambda rb, sbg: lambda v: v.tensor_copy(
                        out=rb[:], in_=sbg[:, 0:64]))(rb, sbg), deps=(ev_rb1,), sig=True)
                else:
                    ev_rbe = p.op("vector", (lambda mb, rb: lambda v: v.tensor_copy(
                        out=rb[:], in_=mb[:, 192:256]))(mb, rb), deps=(ev_rb1,), sig=True)
                bank_r(mname, ev_rbe)
                if DBG and ci == 0 and si == 0 and g == 0:
                    ev_tapN = p.op("vector", (lambda mb: lambda v: v.tensor_copy(
                        out=trA[:, 128:192], in_=mb[:, 128:192]))(mb),
                        deps=[ev_rbe], sig=True)
                    p.op("sync", lambda s: s.dma_start(out=dbg["dbg_np"], in_=trA[:, 128:192]),
                         deps=[ev_tapN], dma="q")
                    bank_r(mname, ev_tapN)
                if PHASE in (226, 227, 228, 229, 231, 232):
                    if DBG and ci == 0 and si == 0 and g == 0:
                        e0 = p.op("sync", (lambda sbg: lambda s: s.dma_start(
                            out=dbg["dbg_vst"], in_=sbg[:, 0:64]))(sbg),
                            deps=ev_e, dma="q")
                        e1 = p.op("sync", (lambda sbg: lambda s: s.dma_start(
                            out=dbg["dbg_sbd"], in_=sbg[:]))(sbg),
                            deps=ev_e + [ev_rbe, e0], dma="q")
                        e3 = p.op("sync", lambda s: s.dma_start(
                            out=dbg["dbg_recd"], in_=recd[:]),
                            deps=[ev_rc, e1], dma="q")
                        e4 = p.op("sync", (lambda rb: lambda s: s.dma_start(
                            out=dbg["dbg_recb"], in_=rb[:]))(rb),
                            deps=[ev_rbe, e3], dma="q")
                        bank_r(mname, e1); bank_r(mname, e4)
                        ringfree[0] = e4
                        grp_done[g] = e4
                    else:
                        grp_done[g] = ev_rbe
                        ringfree[g % 3] = ev_rbe
                    continue
                dst = amid[:, :].rearrange("p (hh t) -> p hh t", hh=NCD)[
                    :, :, t0 + 8 * g:t0 + 8 * g + 8]
                ev_fin = p.op("vector", (lambda mb, rb, dst: lambda v: v.scalar_tensor_tensor(
                    out=dst, in0=mb[:, 128:192].rearrange("p (hh j) -> p hh j", hh=8),
                    scalar=1.0, in1=rb[:, :].rearrange("p (hh j) -> p hh j", hh=8),
                    op0=ALU.mult, op1=ALU.mult))(mb, rb, dst),
                    deps=(ev_rbe,), sig=True)
                bank_r(mname, ev_fin)
                ringfree[g % 3] = ev_fin
                grp_done[g] = ev_fin
                if DBG and ci == 0 and si == 0 and g == 0:
                    e1 = p.op("sync", (lambda sbg: lambda s: s.dma_start(
                        out=dbg["dbg_sbd"], in_=sbg[:]))(sbg), deps=ev_e, dma="q")
                    e2 = p.op("sync", (lambda vt: lambda s: s.dma_start(
                        out=dbg["dbg_vst"], in_=vt[:]))(vt), deps=[ev_vste], dma="q")
                    e3 = p.op("sync", lambda s: s.dma_start(
                        out=dbg["dbg_recd"], in_=recd[:]), deps=[ev_rc], dma="q")
                    e4 = p.op("sync", (lambda rb: lambda s: s.dma_start(
                        out=dbg["dbg_recb"], in_=rb[:]))(rb), deps=[ev_rbe], dma="q")
                    for _e in (e1, e2, e3, e4):
                        ringfree[g % 3] = _e
                        bank_r(mname, _e)
                    ringfree[0] = e4
            ev_amid_last = grp_done[-1]
            if DBG and PHASE in (232, 99) and ci == 0 and si == nslab - 1:
                _dump_srcs = [
                    (sbd[0][:, :], 0), (sbd[1][:, :], 128), (sbd[2][:, :], 256),
                    (recb[0][:, :].bitcast(BF16), 384),
                    (recb[1][:, :].bitcast(BF16), 512),
                    (recb[2][:, :].bitcast(BF16), 640),
                    (vstr[0][:, :], 768), (vstr[1][:, :], 832), (vstr[2][:, :], 896),
                    (recd[:, :].bitcast(BF16), 960),
                    (wbig[1][:, 8192 - 256:8192], 1024),
                    (wbig[0][:, 8192 - 256:8192], 1280),
                ]
                _prev = [d for d in grp_done if d is not None]
                for _src, _c0 in _dump_srcs:
                    _w = _src.shape[1]
                    _prev = [p.op("sync", (lambda _src, _c0, _w: lambda s: s.dma_start(
                        out=dbg["dbg_dump"][:, _c0:_c0 + _w], in_=_src))(_src, _c0, _w),
                        deps=_prev, dma="q")]
            if PHASE in (226, 227, 228, 229, 231, 232):
                ev_qkv_last = grp_done[-1]

        if PHLE(15) or PHLE(2):
            dps = [ev_amid_last, ev_qkv_last]
            ev_store_prev = p.op("sync", (lambda ci: lambda s: s.dma_start(
                out=out_dst[:, :, ci * chunk:(ci + 1) * chunk], in_=xfm[:]))(ci),
                deps=dps, dma="q")
            ev_x_prev_readers = [d for d in dps if d is not None]
            continue

        if DBG and ci == 0:
            p.op("sync", lambda s: s.dma_start(out=dbg["dbg_amid"], in_=amid[:]),
                 deps=[ev_amid_last], dma="q")

        wb, evw, slot = wload(scr["o"].rearrange("(kc pp) m -> pp kc m", pp=128),
                              8 * D, 8, extra_dep=prep_done["o"])
        ev_oev = []
        for oc in range(NCD):
            sl = oc % 4
            ev_mm = None
            for kc in range(NCD):
                ev_mm = p.op("tensor", (lambda oc, kc, wb, sl: lambda t: t.matmul(
                    out=pslot(sl), lhsT=wb[:, kc * D + oc * 128:kc * D + oc * 128 + 128],
                    rhs=amid[:, kc * chunk:(kc + 1) * chunk],
                    start=(kc == 0), stop=(kc == NCD - 1)))(oc, kc, wb, sl),
                    deps=[evw, mmfree[sl], ev_amid_last], sig=(kc == NCD - 1))
            ev_e = p.op("vector", (lambda oc, sl: lambda v: v.scalar_tensor_tensor(
                out=afm[:, oc * chunk:(oc + 1) * chunk], in0=pslot(sl),
                scalar=cb["ob_f"][:, oc:oc + 1],
                in1=hfm[:, oc * chunk:(oc + 1) * chunk],
                op0=ALU.add, op1=ALU.add))(oc, sl),
                deps=[ev_mm, evs_h[oc]] + ev_const, sig=True)
            mmfree[sl] = ev_e
            ev_oev.append(ev_e)
        wfree(slot, ev_oev[-1])

        if PHASE == 240:
            ev_store_prev = p.op("gpsimd", (lambda ci: lambda g_: g_.dma_start(
                out=out_dst[:, :, ci * chunk:(ci + 1) * chunk], in_=amid[:]))(ci),
                deps=[ev_amid_last], dma="g")
            ev_x_prev_readers = [ev_amid_last]
            continue
        if PHASE == 241:
            ev_store_prev = p.op("sync", (lambda ci: lambda s: s.dma_start(
                out=out_dst[:, :, ci * chunk:(ci + 1) * chunk], in_=afm[:]))(ci),
                deps=list(ev_oev), dma="q")
            ev_x_prev_readers = list(ev_oev)
            continue
        if DBG and ci == 0:
            p.op("sync", lambda s: s.dma_start(out=dbg["dbg_afm"], in_=afm[:]),
                 deps=ev_oev, dma="q")

        evs_xn2 = emit_ln(afm, mmin, 0, 0, cb["ln2w_f"], cb["ln2b_f"], ev_oev,
                          extra_dst_deps=[ev_qkv_last])

        ev_gelu = []
        for oc in range(NCI):
            wsrc = scr["fc"][oc * D:(oc + 1) * D, :].rearrange(
                "(kc pp) m -> pp kc m", pp=128)
            wb, evw, slot = wload(wsrc, D, 8, extra_dep=prep_done["fc"])
            sl = oc % 4
            ev_mm = None
            for kc in range(NCD):
                ev_mm = p.op("tensor", (lambda oc, kc, wb, sl: lambda t: t.matmul(
                    out=pslot(sl), lhsT=wb[:, kc * 128:(kc + 1) * 128],
                    rhs=mmin[:, kc * chunk:(kc + 1) * chunk],
                    start=(kc == 0), stop=(kc == NCD - 1)))(oc, kc, wb, sl),
                    deps=[evw, mmfree[sl], evs_xn2[kc]], sig=(kc == NCD - 1))
            ev_e = p.op("scalar", (lambda oc, sl: lambda s: s.activation(
                out=gfm[:, oc * chunk:(oc + 1) * chunk], in_=pslot(sl), func=AF.Gelu,
                bias=cb["fcb_f"][:, oc:oc + 1]))(oc, sl),
                deps=[ev_mm] + ev_const + ([prep_done["d"]] if ci == 0 else []), sig=True)
            mmfree[sl] = ev_e
            ev_gelu.append(ev_e)
            wfree(slot, ev_e)
        ev_projev = []
        for oc in range(NCD):
            wsrc = scr["proj"][oc * INNER:(oc + 1) * INNER, :].rearrange(
                "(kc pp) m -> pp kc m", pp=128)
            wb, evw, slot = wload(wsrc, 4 * D, 32, extra_dep=prep_done["proj"])
            sl = oc % 4
            ev_mm = None
            for kc in range(NCI):
                ev_mm = p.op("tensor", (lambda oc, kc, wb, sl: lambda t: t.matmul(
                    out=pslot(sl), lhsT=wb[:, kc * 128:(kc + 1) * 128],
                    rhs=gfm[:, kc * chunk:(kc + 1) * chunk],
                    start=(kc == 0), stop=(kc == NCI - 1)))(oc, kc, wb, sl),
                    deps=[evw, mmfree[sl], ev_gelu[kc]], sig=(kc == NCI - 1))
            vt = vtmp[oc % 2]
            dep_y = [ev_mm] + ev_const
            if len(ev_projev) >= 2:
                dep_y.append(ev_projev[-2])
            ev_y = p.op("vector", (lambda oc, sl, vt: lambda v: v.scalar_tensor_tensor(
                out=vt[:], in0=pslot(sl), scalar=cb["projb_f"][:, oc:oc + 1],
                in1=afm[:, oc * chunk:(oc + 1) * chunk],
                op0=ALU.add, op1=ALU.add))(oc, sl, vt),
                deps=dep_y, sig=True)
            ev_xu = p.op("vector", (lambda oc, vt: lambda v: v.scalar_tensor_tensor(
                out=xfm[:, oc * chunk:(oc + 1) * chunk], in0=vt[:],
                scalar=mods[:, 16 + oc:16 + oc + 1],
                in1=xfm[:, oc * chunk:(oc + 1) * chunk],
                op0=ALU.mult, op1=ALU.add))(oc, vt),
                deps=(ev_y, ev_mods_done), sig=True)
            mmfree[sl] = ev_xu
            ev_projev.append(ev_xu)
            wfree(slot, ev_y)

        if PHLE(3) or PHASE == 242:
            ev_store_prev = p.op("sync", (lambda ci: lambda s: s.dma_start(
                out=out_dst[:, :, ci * chunk:(ci + 1) * chunk], in_=xfm[:]))(ci),
                deps=ev_projev, dma="q")
            ev_x_prev_readers = list(ev_projev)
            continue

        if DBG and ci == 0:
            p.op("sync", lambda s: s.dma_start(out=dbg["dbg_gfm"], in_=gfm[:]),
                 deps=ev_gelu + [ev_projev[-1]], dma="q")
            p.op("sync", lambda s: s.dma_start(out=dbg["dbg_x1"], in_=xfm[:]),
                 deps=ev_projev, dma="q")

        evs_h2 = emit_ln(xfm, hfm, 32, 24, None, None, ev_projev,
                         extra_dst_deps=ev_oev)

        ev_trmax = None
        ev_trmin = None
        for c in range(1, NCD):
            if c == 1:
                ev_trmax = p.op("vector", lambda v: v.tensor_tensor(
                    out=trA[:], in0=hfm[:, 0:chunk], in1=hfm[:, chunk:2 * chunk],
                    op=ALU.max), deps=[evs_h2[0], evs_h2[1], evs_h2[-1]] +
                    ([("dq0", p.counts["dq0"]), ("dq1", p.counts["dq1"])] if DBG else []),
                    sig=True)
                ev_trmin = p.op("vector", lambda v: v.tensor_tensor(
                    out=vtmp[1][:], in0=hfm[:, 0:chunk], in1=hfm[:, chunk:2 * chunk],
                    op=ALU.min), deps=(evs_h2[0], evs_h2[1], evs_h2[-1], ev_trmax), sig=True)
            else:
                ev_trmax = p.op("vector", (lambda c: lambda v: v.tensor_tensor(
                    out=trA[:], in0=trA[:], in1=hfm[:, c * chunk:(c + 1) * chunk],
                    op=ALU.max))(c), deps=(ev_trmax, evs_h2[c]), sig=True)
                ev_trmin = p.op("vector", (lambda c: lambda v: v.tensor_tensor(
                    out=vtmp[1][:], in0=vtmp[1][:], in1=hfm[:, c * chunk:(c + 1) * chunk],
                    op=ALU.min))(c), deps=(ev_trmin, evs_h2[c]), sig=True)
        ev_tr = p.op("vector", lambda v: v.scalar_tensor_tensor(
            out=trA[:], in0=vtmp[1][:], scalar=-1.0, in1=trA[:],
            op0=ALU.mult, op1=ALU.max), deps=(ev_trmax, ev_trmin), sig=True)
        ev_mr = part_max(ev_tr)
        ev_l = p.op("scalar", lambda s: s.activation(out=rows[3][:], in_=rows[6][:],
                                                     func=AF.Ln), deps=(ev_mr,), sig=True)
        ev_sx = p.op("scalar", lambda s: s.activation(
            out=rows[3][:], in_=rows[3][:], func=AF.Exp, scale=-1.0,
            bias=cl127[:]), deps=(ev_l, ev_cc), sig=True)
        ev_sxb = p.op("tensor", lambda t: t.matmul(
            out=bcp[0][:, 0:chunk], lhsT=onesrow_f[:], rhs=rows[3][:],
            start=True, stop=True), deps=[ev_sx] + bank_w("bc0"), sig=True)
        ev_rg = p.op("vector", lambda v: v.tensor_scalar(
            out=rows[5][:], in0=rows[6][:], scalar1=srow[:, 0:1], scalar2=1.0 / 127.0,
            op0=ALU.mult, op1=ALU.mult), deps=(ev_mr, prep_done["g"]), sig=True)
        ev_rgb = p.op("tensor", lambda t: t.matmul(
            out=bcp[1][:, 0:chunk], lhsT=onesrow_f[:], rhs=rows[5][:],
            start=True, stop=True), deps=[ev_rg] + bank_w("bc1"), sig=True)
        ev_rgt = p.op("scalar", lambda s: s.activation(
            out=rsxg_t[:], in_=bcp[1][:, 0:chunk], func=AF.Identity),
            deps=(ev_rgb,), sig=True)
        bank_r("bc1", ev_rgt)
        ev_keep = p.op("vector", lambda v: v.tensor_copy(out=rows[4][:], in_=rows[6][:]),
                       deps=(ev_mr,), sig=True)
        evs_xq = []
        for c in range(NCD):
            dq = [ev_sxb, evs_h2[c], ev_gelu[-1]]
            if evs_xq:
                dq.append(evs_xq[-1])
            ev_q1 = p.op("vector", (lambda c: lambda v: v.tensor_tensor(
                out=qi[:], in0=hfm[:, c * chunk:(c + 1) * chunk], in1=bcp[0][:, 0:chunk],
                op=ALU.mult))(c), deps=dq, sig=True)
            bank_r("bc0", ev_q1)
            ev_q2 = p.op("vector", (lambda c: lambda v: v.tensor_copy(
                out=mmin[:, c * chunk:(c + 1) * chunk], in_=qi[:]))(c),
                deps=(ev_q1,), sig=True)
            evs_xq.append(ev_q2)

        if PHASE == 243:
            ev_store_prev = p.op("sync", (lambda ci: lambda s: s.dma_start(
                out=out_dst[:, :, ci * chunk:(ci + 1) * chunk], in_=hfm[:]))(ci),
                deps=evs_h2 + [ev_trmin, ev_tr], dma="q")
            ev_x_prev_readers = evs_h2
            continue
        if PHASE == 244:
            ev_store_prev = p.op("gpsimd", (lambda ci: lambda g_: g_.dma_start(
                out=out_dst[:, :, ci * chunk:(ci + 1) * chunk], in_=mmin[:]))(ci),
                deps=list(evs_xq), dma="g")
            ev_x_prev_readers = list(evs_xq)
            continue
        if DBG and ci == 0:
            p.op("sync", lambda s: s.dma_start(out=dbg["dbg_h2"], in_=hfm[:]),
                 deps=evs_h2 + [ev_trmin], dma="q")
            p.op("sync", lambda s: s.dma_start(out=dbg["dbg_xq"], in_=mmin[:]),
                 deps=evs_xq, dma="q")

        ev_tr2 = None
        ev_prod = []
        for oc in range(NCI):
            wg_src = scr["g"][oc * D:(oc + 1) * D, :].rearrange("(kc pp) m -> pp kc m", pp=128)
            wu_src = scr["u"][oc * D:(oc + 1) * D, :].rearrange("(kc pp) m -> pp kc m", pp=128)
            wbg_, evwg, slotg = wload(wg_src, D, 8, extra_dep=prep_done["g"])
            wbu_, evwu, slotu = wload(wu_src, D, 8, extra_dep=prep_done["u"])
            slg = oc % 2
            slu = 2 + oc % 2
            ev_mg = None
            ev_muu = None
            for kc in range(NCD):
                ev_mg = p.op("tensor", (lambda oc, kc, wbg_, slg: lambda t: t.matmul(
                    out=pslot(slg), lhsT=wbg_[:, kc * 128:(kc + 1) * 128],
                    rhs=mmin[:, kc * chunk:(kc + 1) * chunk],
                    start=(kc == 0), stop=(kc == NCD - 1)))(oc, kc, wbg_, slg),
                    deps=[evwg, mmfree[slg], evs_xq[kc]], sig=(kc == NCD - 1))
            for kc in range(NCD):
                ev_muu = p.op("tensor", (lambda oc, kc, wbu_, slu: lambda t: t.matmul(
                    out=pslot(slu), lhsT=wbu_[:, kc * 128:(kc + 1) * 128],
                    rhs=mmin[:, kc * chunk:(kc + 1) * chunk],
                    start=(kc == 0), stop=(kc == NCD - 1)))(oc, kc, wbu_, slu),
                    deps=[evwu, mmfree[slu], evs_xq[kc]], sig=(kc == NCD - 1))
            gt = gtr[oc % 2]
            sg = sigr[oc % 2]
            gu = gur[oc % 2]
            dep_p1 = [ev_mg, ev_rgt]
            if len(ev_prod) >= 2:
                dep_p1.append(ev_prod[-2])
            ev_p1 = p.op("vector", (lambda slg, gt: lambda v: v.tensor_tensor(
                out=gt[:], in0=pslot(slg), in1=rsxg_t[:], op=ALU.mult))(slg, gt),
                deps=dep_p1, sig=True)
            ev_p2 = p.op("scalar", (lambda gt, sg: lambda s: s.activation(
                out=sg[:], in_=gt[:], func=AF.Sigmoid))(gt, sg), deps=(ev_p1,), sig=True)
            ev_p3 = p.op("vector", (lambda slu, gt, gu: lambda v: v.tensor_tensor(
                out=gu[:], in0=pslot(slu), in1=gt[:], op=ALU.mult))(slu, gt, gu),
                deps=(ev_muu, ev_p1), sig=True)
            dep_p4 = [ev_p2, ev_p3]
            if oc == 0:
                dep_p4.append(ev_projev[-1])
            ev_p4 = p.op("vector", (lambda oc, gu, sg: lambda v: v.tensor_tensor(
                out=gfm[:, oc * chunk:(oc + 1) * chunk], in0=gu[:], in1=sg[:],
                op=ALU.mult))(oc, gu, sg), deps=dep_p4, sig=True)
            mmfree[slg] = ev_p3
            mmfree[slu] = ev_p3
            wfree(slotg, ev_mg)
            wfree(slotu, ev_muu)
            ev_prod.append(ev_p4)
            if oc == 1:
                ev_tr2 = p.op("vector", lambda v: v.tensor_tensor(
                    out=trAb[:], in0=gfm[:, 0:chunk], in1=gfm[:, chunk:2 * chunk],
                    op=ALU.max), deps=(ev_prod[0], ev_p4, ev_mr), sig=True)
                ev_tr2n = p.op("vector", lambda v: v.tensor_tensor(
                    out=trBb[:], in0=gfm[:, 0:chunk], in1=gfm[:, chunk:2 * chunk],
                    op=ALU.min), deps=(ev_prod[0], ev_p4, ev_tr2), sig=True)
            elif oc > 1:
                ev_tr2 = p.op("vector", (lambda oc: lambda v: v.tensor_tensor(
                    out=trAb[:], in0=trAb[:], in1=gfm[:, oc * chunk:(oc + 1) * chunk],
                    op=ALU.max))(oc), deps=(ev_tr2, ev_p4), sig=True)
                ev_tr2n = p.op("vector", (lambda oc: lambda v: v.tensor_tensor(
                    out=trBb[:], in0=trBb[:], in1=gfm[:, oc * chunk:(oc + 1) * chunk],
                    op=ALU.min))(oc), deps=(ev_tr2n, ev_p4), sig=True)

        ev_tr2c = p.op("vector", lambda v: v.scalar_tensor_tensor(
            out=trA[:], in0=trBb[:], scalar=-1.0, in1=trAb[:],
            op0=ALU.mult, op1=ALU.max), deps=(ev_tr2, ev_tr2n), sig=True)
        ev_mr2 = part_max(ev_tr2c)
        ev_l2 = p.op("scalar", lambda s: s.activation(out=rows[3][:], in_=rows[6][:],
                                                      func=AF.Ln), deps=(ev_mr2,), sig=True)
        ev_sx2 = p.op("scalar", lambda s: s.activation(
            out=rows[3][:], in_=rows[3][:], func=AF.Exp, scale=-1.0,
            bias=cl127[:]), deps=(ev_l2, ev_cc), sig=True)
        ev_sxb2 = p.op("tensor", lambda t: t.matmul(
            out=bcp[0][:, 0:chunk], lhsT=onesrow_f[:], rhs=rows[3][:],
            start=True, stop=True), deps=[ev_sx2] + bank_w("bc0"), sig=True)
        ev_sxt2 = p.op("scalar", lambda s: s.activation(
            out=sxb2_t[:], in_=bcp[0][:, 0:chunk], func=AF.Identity),
            deps=(ev_sxb2,), sig=True)
        bank_r("bc0", ev_sxt2)
        ev_sc1 = p.op("vector", lambda v: v.tensor_tensor(
            out=rows[5][:], in0=rows[4][:], in1=rows[6][:], op=ALU.mult),
            deps=(ev_keep, ev_mr2), sig=True)
        ev_sc2 = p.op("vector", lambda v: v.tensor_scalar(
            out=rows[5][:], in0=rows[5][:], scalar1=srow[:, 1:2],
            scalar2=1.0 / (127.0 * 127.0), op0=ALU.mult, op1=ALU.mult),
            deps=(ev_sc1, prep_done["u"]), sig=True)
        ev_sc3 = p.op("vector", lambda v: v.tensor_scalar(
            out=rows[5][:], in0=rows[5][:], scalar1=srow[:, 2:3], scalar2=None,
            op0=ALU.mult), deps=(ev_sc2, prep_done["d"]), sig=True)
        ev_scb = p.op("tensor", lambda t: t.matmul(
            out=bcp[1][:, 0:chunk], lhsT=onesrow_f[:], rhs=rows[5][:],
            start=True, stop=True), deps=[ev_sc3] + bank_w("bc1"), sig=True)
        ev_sct = p.op("scalar", lambda s: s.activation(
            out=scaleb_t[:], in_=bcp[1][:, 0:chunk], func=AF.Identity),
            deps=(ev_scb,), sig=True)
        bank_r("bc1", ev_sct)

        evs_pq = []
        for oc in range(NCI):
            dq = [ev_sxt2, ev_prod[oc], ev_tr2]
            if evs_pq:
                dq.append(evs_pq[-1])
            ev_q1 = p.op("vector", (lambda oc: lambda v: v.tensor_tensor(
                out=qi[:], in0=gfm[:, oc * chunk:(oc + 1) * chunk], in1=sxb2_t[:],
                op=ALU.mult))(oc), deps=dq, sig=True)
            ev_q2 = p.op("vector", (lambda oc: lambda v: v.tensor_copy(
                out=gfm[:, oc * chunk:(oc + 1) * chunk], in_=qi[:]))(oc),
                deps=(ev_q1,), sig=True)
            evs_pq.append(ev_q2)

        if PHLE(4):
            ev_store_prev = p.op("sync", (lambda ci: lambda s: s.dma_start(
                out=out_dst[:, :, ci * chunk:(ci + 1) * chunk], in_=xfm[:]))(ci),
                deps=[evs_pq[-1]], dma="q")
            ev_x_prev_readers = [evs_pq[-1]]
            continue

        if PHASE == 245:
            ev_store_prev = p.op("gpsimd", (lambda ci: lambda g_: g_.dma_start(
                out=out_dst[:, :, ci * chunk:(ci + 1) * chunk],
                in_=gfm[:, 0:NCD * chunk]))(ci),
                deps=list(evs_pq), dma="g")
            ev_x_prev_readers = list(evs_pq)
            continue
        if DBG and ci == 0:
            p.op("sync", lambda s: s.dma_start(out=dbg["dbg_pq"], in_=gfm[:]),
                 deps=evs_pq, dma="q")

        if PHASE == 247:
            ev_store_prev = p.op("gpsimd", (lambda: lambda g_: g_.dma_start(
                out=out_fm[0:1024, 0:128],
                in_=scr["d"][SCRD_OFF:SCRD_OFF + 1024, 0:128]))(),
                deps=[prep_done["d"], evs_pq[-1]], dma="g")
            ev_x_prev_readers = [evs_pq[-1]]
            continue
        if PHASE == 246:
            ev_store_prev = p.op("sync", (lambda ci: lambda s: s.dma_start(
                out=out_dst[:, 0:1, ci * chunk:(ci + 1) * chunk],
                in_=scaleb_t[:].rearrange("p (c t) -> p c t", c=1)))(ci),
                deps=[ev_sct] + list(evs_pq), dma="q")
            ev_x_prev_readers = [ev_sct]
            continue

        ev_downev = []
        ev_st250 = []
        ev_yh = []
        for oc in range(1 if PHASE == 248 else NCD):
            wsrc = scr["d"][oc * INNER:(oc + 1) * INNER, :].rearrange(
                "(kc pp) m -> pp kc m", pp=128)
            wb, evw, slot = wload(wsrc, 4 * D, 32, extra_dep=prep_done["d"])
            sl = oc % 4
            ev_mm = None
            for kc in range(NCI):
                ev_mm = p.op("tensor", (lambda oc, kc, wb, sl: lambda t: t.matmul(
                    out=pslot(sl), lhsT=wb[:, kc * 128:(kc + 1) * 128],
                    rhs=gfm[:, kc * chunk:(kc + 1) * chunk],
                    start=(kc == 0), stop=(kc == NCI - 1)))(oc, kc, wb, sl),
                    deps=[evw, mmfree[sl], evs_pq[kc]], sig=(kc == NCI - 1))
            vt = vtmp[oc % 2]
            dep_y = [ev_mm, ev_sct, ev_mods_done]
            if len(ev_downev) >= 2:
                dep_y.append(ev_downev[-2])
            if PHASE in (250, 251) and len(ev_st250) >= 2:
                dep_y.append(ev_st250[-2])
            if PHASE == 252 and ev_downev:
                dep_y.append(ev_downev[-1])
            if PHASE == 254:
                if len(ev_yh) >= 2:
                    dep_y.append(ev_yh[-2])
                ev_yc = p.op("scalar", (lambda sl, vt: lambda s: s.activation(
                    out=vt[:], in_=pslot(sl), func=AF.Identity))(sl, vt),
                    deps=dep_y, sig=True)
                ev_y = p.op("vector", (lambda oc, vt: lambda v: v.scalar_tensor_tensor(
                    out=trA[:], in0=vt[:], scalar=mods[:, 40 + oc:40 + oc + 1],
                    in1=scaleb_t[:], op0=ALU.mult, op1=ALU.mult))(oc, vt),
                    deps=(ev_yc,), sig=True)
                ev_yh.append(ev_y)
                ev_xu = p.op("vector", (lambda oc: lambda v: v.tensor_tensor(
                    out=xfm[:, oc * chunk:(oc + 1) * chunk],
                    in0=xfm[:, oc * chunk:(oc + 1) * chunk], in1=trA[:],
                    op=ALU.add))(oc), deps=(ev_y,), sig=True)
                mmfree[sl] = ev_xu
                ev_downev.append(ev_xu)
                wfree(slot, ev_yc)
                continue
            ev_y = p.op("vector", (lambda oc, sl, vt: lambda v: v.scalar_tensor_tensor(
                out=vt[:], in0=pslot(sl), scalar=mods[:, 40 + oc:40 + oc + 1],
                in1=scaleb_t[:], op0=ALU.mult, op1=ALU.mult))(oc, sl, vt),
                deps=dep_y, sig=True)
            if PHASE in (250, 251):
                ev_st250.append(p.op("sync", (lambda ci, oc, vt: lambda s: s.dma_start(
                    out=out_dst[:, oc:oc + 1, ci * chunk:(ci + 1) * chunk],
                    in_=vt[:].rearrange("p (c t) -> p c t", c=1)))(ci, oc, vt),
                    deps=[ev_y], dma="q"))
            ev_xu = p.op("vector", (lambda oc, vt: lambda v: v.tensor_tensor(
                out=xfm[:, oc * chunk:(oc + 1) * chunk],
                in0=xfm[:, oc * chunk:(oc + 1) * chunk], in1=vt[:], op=ALU.add))(oc, vt),
                deps=(ev_y,), sig=True)
            mmfree[sl] = ev_xu
            ev_downev.append(ev_xu)
            wfree(slot, ev_y)
            if PHASE == 248 and oc == 0:
                ev_store_248 = p.op("sync", (lambda ci, vt: lambda s: s.dma_start(
                    out=out_dst[:, 0:1, ci * chunk:(ci + 1) * chunk],
                    in_=vt[:].rearrange("p (c t) -> p c t", c=1)))(ci, vt),
                    deps=[ev_y, ev_xu], dma="q")

        if PHASE == 248:
            ev_store_prev = ev_store_248
            ev_x_prev_readers = list(ev_downev)
            continue
        if PHASE == 250:
            ev_store_prev = ev_st250[-1]
            ev_x_prev_readers = list(ev_downev)
            continue
        if PHASE == 251:
            ev_store_prev = p.op("sync", (lambda ci: lambda s: s.dma_start(
                out=out_dst[:, :, ci * chunk:(ci + 1) * chunk], in_=xfm[:]))(ci),
                deps=ev_downev + [ev_st250[-1], ev_st250[-2]], dma="q")
            ev_x_prev_readers = list(ev_downev)
            continue
        ev_store_prev = p.op("sync", (lambda ci: lambda s: s.dma_start(
            out=out_dst[:, :, ci * chunk:(ci + 1) * chunk], in_=xfm[:]))(ci),
            deps=ev_downev, dma="q")
        ev_x_prev_readers = list(ev_downev)

    p.op("sync", lambda s: None, deps=[ev_store_prev])

    with nc.Block() as block:
        def emitter(eng_name):
            def run(eng):
                waited = {}
                for emit, deps, incs in p.ops[eng_name]:
                    for (s, v) in deps:
                        if waited.get(s, 0) < v:
                            eng.wait_ge(sems[s], v)
                            waited[s] = v
                    inst = emit(eng)
                    if inst is not None:
                        for (s, amt) in incs:
                            inst.then_inc(sems[s], amt)
            return run

        block.sync(emitter("sync"))
        block.tensor(emitter("tensor"))
        block.vector(emitter("vector"))
        block.scalar(emitter("scalar"))
        block.gpsimd(emitter("gpsimd"))

    ctx.close()
    return nc



def _host_inputs(inputs, b, ttot):
    f32 = np.float32
    x = np.ascontiguousarray(np.asarray(inputs["L0_attn_x"][b], dtype=f32)[:ttot, :].T)

    def T(w):
        return np.ascontiguousarray(np.asarray(w, dtype=f32).T)

    def blocked(wT, nb):
        i_, o_ = wT.shape
        return np.ascontiguousarray(
            wT.reshape(i_, nb, 128).transpose(1, 0, 2).reshape(nb * i_, 128))

    qT = T(inputs["L0_attn_q_w"]); kT = T(inputs["L0_attn_k_w"])
    vT = T(inputs["L0_attn_v_w"]); oT = T(inputs["L0_attn_o_w"])
    fcT = T(inputs["L0_attn_fc_w"])
    projT = T(inputs["L0_attn_proj_w"])
    gT = T(inputs["L0_ffn_w_gate"]); uT = T(inputs["L0_ffn_w_up"])
    dT = T(inputs["L0_ffn_w_down"])

    fm = lambda vec, ncol: np.ascontiguousarray(
        np.asarray(vec, dtype=f32).reshape(ncol, 128).T)

    mask = np.zeros((128, 128), dtype=ml_dtypes.bfloat16)
    for j in range(8):
        mask[16 * j:16 * (j + 1), 16 * j:16 * (j + 1)] = 1
    ident64 = np.eye(64, dtype=ml_dtypes.bfloat16)
    ident128f = np.eye(128, dtype=f32)

    return {
        "x_fm": x,
        "qT": qT, "kT": kT, "vT": vT, "oT": oT,
        "fcB": blocked(fcT, NCI), "projB": blocked(projT, NCD),
        "gB": blocked(gT, NCI), "uB": blocked(uT, NCI), "dB": blocked(dT, NCD),
        "wmod": np.ascontiguousarray(np.asarray(inputs["L0_wmod"], dtype=f32)),
        "cfm": fm(np.asarray(inputs["L0_c"], dtype=f32)[b], NCD),
        "qb_r": np.ascontiguousarray(np.asarray(inputs["L0_attn_q_b"], dtype=f32).reshape(1, D)),
        "kb_r": np.ascontiguousarray(np.asarray(inputs["L0_attn_k_b"], dtype=f32).reshape(1, D)),
        "vb_r": np.ascontiguousarray(np.asarray(inputs["L0_attn_v_b"], dtype=f32).reshape(1, D)),
        "ob_f": fm(inputs["L0_attn_o_b"], NCD),
        "projb_f": fm(inputs["L0_attn_proj_b"], NCD),
        "fcb_f": fm(inputs["L0_attn_fc_b"], NCI),
        "ln1w_f": fm(inputs["L0_attn_ln1_w"], NCD),
        "ln1b_f": fm(inputs["L0_attn_ln1_b"], NCD),
        "ln2w_f": fm(inputs["L0_attn_ln2_w"], NCD),
        "ln2b_f": fm(inputs["L0_attn_ln2_b"], NCD),
        "mask128": mask, "ident64": ident64, "ident128f": ident128f,
    }


def kernel(**inputs):
    ttot = int(inputs["L0_attn_x"].shape[1])
    ncores = int(inputs["L0_attn_x"].shape[0])
    nc = build_nc(ttot=ttot)
    in_maps = [_host_inputs(inputs, b, ttot) for b in range(ncores)]
    res = run_bass_kernel_spmd(nc, in_maps, core_ids=list(range(ncores)))
    outs = []
    for b in range(ncores):
        ofm = np.asarray(res.results[b]["out_fm"])
        outs.append(np.ascontiguousarray(ofm.T))
    return np.stack(outs, axis=0).astype(np.float32)


# version 04
# speedup vs baseline: 1.0494x; 1.0494x over previous
import numpy as np
import ml_dtypes
from contextlib import ExitStack

import concourse.bass as bass
import concourse.mybir as mybir
from concourse.bass_utils import run_bass_kernel_spmd

F32 = mybir.dt.float32
BF16 = mybir.dt.bfloat16
I32 = mybir.dt.int32
AF = mybir.ActivationFunctionType
ALU = mybir.AluOpType
AX = mybir.AxisListType.X

D = 1024
H = 16
HD = 64
INNER = 4096
NCD = D // 128
NCI = INNER // 128
EPS = 1e-5

TTOT = 4096
PHASE = 99
DBG = False
CHUNK = 512
SCRD_OFF = 0

_PH_ORD = {1: 10, 2: 20, 15: 30, 21: 40, 22: 50, 224: 60, 225: 70,
           23: 80, 3: 90, 4: 100}


def PHLE(x):
    return _PH_ORD.get(PHASE, 10 ** 9) <= _PH_ORD[x]
SLAB = 256
GRP = 8

ENG_SEM = {"tensor": "sT", "vector": "sV", "scalar": "sS", "gpsimd": "sG", "sync": "sQ"}


DMA_SEMS = {"q": ("dq0", "dq1"), "g": ("dg0", "dg1")}


class Prog:
    def __init__(self):
        self.ops = {e: [] for e in ("sync", "tensor", "vector", "scalar", "gpsimd")}
        self.counts = {s: 0 for s in ("sT", "sV", "sS", "sG", "sQ",
                                      "dq0", "dq1", "dg0", "dg1")}
        self.dma_rr = {"q": 0, "g": 0}

    def op(self, engine, emit, deps=(), sig=False, dma=None):
        incs = []
        ev = None
        if dma is not None:
            pool = DMA_SEMS[dma]
            s = pool[self.dma_rr[dma] % len(pool)]
            self.dma_rr[dma] += 1
            deps = list(deps) + [(s, self.counts[s])]
            self.counts[s] += 16
            incs.append((s, 16))
            ev = (s, self.counts[s])
        elif sig:
            s = ENG_SEM[engine]
            self.counts[s] += 1
            incs.append((s, 1))
            ev = (s, self.counts[s])
        dd = {}
        for d in deps:
            if d is not None:
                dd[d[0]] = max(dd.get(d[0], 0), d[1])
        self.ops[engine].append((emit, sorted(dd.items()), incs))
        return ev


def build_nc(ttot=TTOT, chunk=CHUNK, slab=SLAB):
    nch = ttot // chunk
    nslab = chunk // slab
    ngrp = slab // GRP

    nc = bass.Bass("TRN2", target_bir_lowering=False)
    p = Prog()

    dr = {}

    def din(name, shape, dt=F32):
        dr[name] = nc.dram_tensor(name, list(shape), dt, kind="ExternalInput").ap()
        return dr[name]

    din("x_fm", [D, ttot])
    din("qT", [D, D]); din("kT", [D, D]); din("vT", [D, D]); din("oT", [D, D])
    din("fcB", [NCI * D, 128]); din("gB", [NCI * D, 128]); din("uB", [NCI * D, 128])
    din("projB", [NCD * INNER, 128]); din("dB", [NCD * INNER, 128])
    din("wmod", [D, 6 * D])
    din("cfm", [128, NCD])
    din("qb_r", [1, D]); din("kb_r", [1, D]); din("vb_r", [1, D])
    din("ob_f", [128, NCD]); din("projb_f", [128, NCD]); din("fcb_f", [128, NCI])
    din("ln1w_f", [128, NCD]); din("ln1b_f", [128, NCD])
    din("ln2w_f", [128, NCD]); din("ln2b_f", [128, NCD])
    din("mask128", [128, 128], BF16)
    din("ident64", [64, 64], BF16)
    din("ident128f", [128, 128], F32)
    out_fm = nc.dram_tensor("out_fm", [D, ttot], F32, kind="ExternalOutput").ap()
    dbg = {}
    if DBG:
        for nm, shp, dt in (("dbg_h", [128, NCD * chunk], F32),
                            ("dbg_xn", [128, NCD * chunk], BF16),
                            ("dbg_kq", [64, H * slab], BF16),
                            ("dbg_kk", [64, H * slab], BF16),
                            ("dbg_kv", [64, H * slab], BF16),
                            ("dbg_amid", [128, NCD * chunk], BF16),
                            ("dbg_afm", [128, NCD * chunk], F32),
                            ("dbg_gfm", [128, NCI * chunk], BF16),
                            ("dbg_x1", [128, NCD * chunk], F32),
                            ("dbg_h2", [128, NCD * chunk], F32),
                            ("dbg_xq", [128, NCD * chunk], BF16),
                            ("dbg_pq", [128, NCI * chunk], BF16),
                            ("dbg_mods", [128, 48], F32),
                            ("dbg_sbd", [128, 128], BF16),
                            ("dbg_vst", [128, 64], BF16),
                            ("dbg_recd", [128, 32], F32),
                            ("dbg_recb", [128, 64], F32),
                            ("dbg_sp", [128, 128], F32),
                            ("dbg_np", [128, 64], F32),
                            ("dbg_dump", [128, 2048], BF16),
                            ("dbg_mask", [128, 128], BF16)):
            dbg[nm] = nc.dram_tensor(nm, shp, dt, kind="ExternalOutput").ap()

    scr = {}
    for nm, shape in (("q", [D, D]), ("k", [D, D]), ("v", [D, D]), ("o", [D, D]),
                      ("fc", [NCI * D, 128]), ("g", [NCI * D, 128]), ("u", [NCI * D, 128]),
                      ("proj", [NCD * INNER, 128]), ("d", [NCD * INNER, 128])):
        scr[nm] = nc.dram_tensor("scr_" + nm, shape, BF16, kind="Internal").ap()

    ctx = ExitStack()
    sb = lambda name, shape, dt=F32: ctx.enter_context(nc.sbuf_tensor(name, list(shape), dt))
    psal = lambda name, shape, dt=F32: ctx.enter_context(nc.psum_tensor(name, list(shape), dt))

    xfm = sb("xfm", [128, NCD * chunk])
    hfm = sb("hfm", [128, NCD * chunk])
    afm = sb("afm", [128, NCD * chunk])
    mmin = sb("mmin", [128, NCD * chunk], BF16)
    khq = sb("khq", [64, H * slab], BF16)
    khk = sb("khk", [64, H * slab], BF16)
    khv = sb("khv", [64, H * slab], BF16)
    amid = sb("amid", [128, NCD * chunk], BF16)
    xbf = amid
    gfm = sb("gfm", [128, NCI * chunk], BF16)
    x2bf = sb("x2bf", [128, NCD * chunk], BF16)
    wbig = [sb("wbigA", [128, 8 * D], BF16), sb("wbigB", [128, 8 * D], BF16)]

    if PHASE == 231:
        recb = [sb("recb%d" % i, [128, 64]) for i in range(3)]
        sbd = [sb("sbd%d" % i, [128, 128], BF16) for i in range(3)]
        vstr = [sb("vst%d" % i, [128, 64], BF16) for i in range(3)]
    else:
        sbd = [sb("sbd%d" % i, [128, 128], BF16) for i in range(3)]
        recb = [sb("recb%d" % i, [128, 64]) for i in range(3)]
        vstr = [sb("vst%d" % i, [128, 64], BF16) for i in range(3)]
    recd = sb("recd", [128, ngrp])
    rbjunk = sb("rbjunk", [128, 64]) if PHASE == 228 else None
    gtr = [sb("gt%d" % i, [128, chunk]) for i in range(2)]
    sigr = [sb("sig%d" % i, [128, chunk], BF16) for i in range(2)]
    gur = [sb("gu%d" % i, [128, chunk]) for i in range(2)]
    Er = [gtr[i][:, :].bitcast(BF16)[0:64, 0:4 * slab] for i in range(2)]
    EBr = [gur[i][:, :].bitcast(BF16)[0:64, 0:4 * slab] for i in range(2)]
    vtmp = [sb("vtmp%d" % i, [128, chunk]) for i in range(2)]
    qi = vtmp[0][:, :].bitcast(I32)
    trA = sb("trA", [128, chunk])
    trAb = x2bf[:, 0:chunk]
    trBb = x2bf[:, chunk:2 * chunk]
    maxcol = sb("maxcol", [128, 4])
    rsxg_t = sb("rsxg_t", [128, chunk])
    sxb2_t = sb("sxb2_t", [128, chunk])
    scaleb_t = sb("scaleb_t", [128, chunk])
    rows = [sb("row%d" % i, [1, chunk]) for i in range(7)]
    srow = sb("srow", [1, 8])
    swb = sb("swb", [128, 4])
    cb = {}
    for nm, shp, dt in (("ob_f", [128, NCD], F32), ("projb_f", [128, NCD], F32),
                        ("fcb_f", [128, NCI], F32),
                        ("ln1w_f", [128, NCD], F32), ("ln1b_f", [128, NCD], F32),
                        ("ln2w_f", [128, NCD], F32), ("ln2b_f", [128, NCD], F32),
                        ("mask128", [128, 128], BF16), ("ident64", [64, 64], BF16),
                        ("ident128f", [128, 128], F32), ("cfm", [128, NCD], F32)):
        cb[nm] = sb("c_" + nm, shp, dt)
    qbb = sb("qbb", [1, D], BF16)
    kbb = sb("kbb", [1, D], BF16)
    vbb = sb("vbb", [1, D], BF16)
    onescol_b = sb("onescol_b", [128, 1], BF16)
    onesrow_f = sb("onesrow_f", [1, 128])
    onesrow_b = sb("onesrow_b", [1, chunk], BF16)
    onecol_f = sb("onecol_f", [128, 1])
    ceps = sb("ceps", [1, 1])
    cl127 = sb("cl127", [1, 1])
    mods = sb("mods", [128, 48])
    siluc = sb("siluc", [128, NCD])
    m8 = sb("m8", [128, 8])
    prep_b = gfm[:, 0:2048]
    prep_i = gfm[:, 4096:8192].bitcast(I32)
    prefA = gfm[:, 8192:12288].bitcast(F32)
    prefB = gfm[:, 12288:16384].bitcast(F32)
    prep_r = sb("prep_r", [128, 18])

    pP = [psal("pA", [128, 1024]), psal("pB", [128, 1024])]
    bcp = [psal("bcA", [128, 512]), psal("bcB", [128, 512])]
    midp = [psal("midA", [128, 512]), psal("midB", [128, 512])]
    vst_ps = [midp[0][:, 264:296].bitcast(BF16), midp[1][:, 264:296].bitcast(BF16)]

    sems = {s: ctx.enter_context(nc.semaphore(s)) for s in
            ("sT", "sV", "sS", "sG", "sQ", "dq0", "dq1", "dg0", "dg1")}

    BT = {"bc0": [], "bc1": [], "mid0": [], "mid1": []}

    def bank_w(name):
        d = BT[name]
        BT[name] = []
        return d

    def bank_r(name, ev):
        if ev is not None:
            BT[name].append(ev)

    for nm in cb:
        p.op("sync", (lambda nm: lambda s: s.dma_start(out=cb[nm][:], in_=dr[nm]))(nm),
             dma="q")
    ev_const = [("dq0", p.counts["dq0"]), ("dq1", p.counts["dq1"])]

    if DBG and PHASE in (232, 99):
        ev_sent = p.op("vector", lambda v: v.memset(mmin[:, 0:128], 7.0), sig=True)
        p.op("sync", lambda s: s.dma_start(out=dbg["dbg_sbd"], in_=mmin[:, 0:128]),
             deps=[ev_sent], dma="q")
    ev_ones = p.op("vector", lambda v: v.memset(onescol_b[:], 1.0), sig=True)
    ev_onesrf = p.op("vector", lambda v: v.memset(onesrow_f[:], 1.0), sig=True)
    ev_onesrb = p.op("vector", lambda v: v.memset(onesrow_b[:], 1.0), sig=True)
    ev_ones2 = p.op("vector", lambda v: v.memset(onecol_f[:], 1.0), sig=True)
    p.op("vector", lambda v: v.memset(ceps[:], EPS))
    ev_cc = p.op("vector", lambda v: v.memset(cl127[:], float(np.log(127.0))), sig=True)
    ev_bb = None
    for nm_, dst in (("qb_r", qbb), ("kb_r", kbb), ("vb_r", vbb)):
        ev_bb = p.op("gpsimd", (lambda nm_, dst: lambda g: g.dma_start(
            out=dst[:], in_=dr[nm_]))(nm_, dst), dma="g")

    ev_silu = p.op("scalar", lambda s: s.activation(out=siluc[:], in_=cb["cfm"][:],
                                                    func=AF.Silu),
                   deps=ev_const, sig=True)
    ev_mod_mm = None
    wmod_hist = []
    for kc in range(8):
        for part in range(3):
            buf = prefA if (kc * 3 + part) % 2 == 0 else prefB
            evd = p.op("sync", (lambda kc, part, buf: lambda s: s.dma_start(
                out=buf[:], in_=dr["wmod"][128 * kc:128 * (kc + 1),
                                           2048 * part:2048 * (part + 1)]))(kc, part, buf),
                deps=(wmod_hist[-2] if len(wmod_hist) >= 2 else None,), dma="q")
            last = None
            for m in range(16):
                mg = part * 16 + m
                sig = (m == 15)
                last = p.op("tensor", (lambda m, mg, kc, buf: lambda t: t.matmul(
                    out=bcp[0][:, mg:mg + 1], lhsT=buf[:, 128 * m:128 * (m + 1)],
                    rhs=siluc[:, kc:kc + 1],
                    start=(kc == 0 and mg == 0), stop=(kc == 7 and mg == 47)))(m, mg, kc, buf),
                    deps=[evd, ev_silu] + (bank_w("bc0") if (kc == 0 and mg == 0) else []),
                    sig=sig)
                if kc == 7 and mg == 47:
                    ev_mod_mm = last
            wmod_hist.append(last)
    if ev_mod_mm is None:
        ev_mod_mm = wmod_hist[-1]
    ev_mods = p.op("vector", lambda v: v.tensor_copy(out=mods[:], in_=bcp[0][:, 0:48]),
                   deps=(ev_mod_mm,), sig=True)
    bank_r("bc0", ev_mods)
    ev_ca1 = p.op("vector", lambda v: v.tensor_scalar(out=mods[:, 8:16], in0=mods[:, 8:16],
                                                      scalar1=1.0, scalar2=None, op0=ALU.add),
                  deps=(ev_mods,), sig=True)
    ev_cb1 = p.op("vector", lambda v: v.tensor_scalar(out=mods[:, 32:40], in0=mods[:, 32:40],
                                                      scalar1=1.0, scalar2=None, op0=ALU.add),
                  deps=(ev_mods,), sig=True)
    ev_g = ev_mods
    for col in (16, 40):
        ev_s0 = p.op("scalar", (lambda col: lambda s: s.activation(
            out=m8[:], in_=mods[:, col:col + 8], func=AF.Sign))(col),
            deps=(ev_g,), sig=True)
        ev_sq = p.op("vector", lambda v: v.scalar_tensor_tensor(
            out=trA[:, 0:8], in0=m8[:], scalar=1.0, in1=m8[:],
            op0=ALU.mult, op1=ALU.mult), deps=(ev_s0,), sig=True)
        ev_sq2 = p.op("vector", lambda v: v.tensor_scalar(
            out=trA[:, 0:8], in0=trA[:, 0:8], scalar1=-1.0, scalar2=1.0,
            op0=ALU.mult, op1=ALU.add), deps=(ev_sq,), sig=True)
        ev_sgn = p.op("vector", lambda v: v.tensor_tensor(
            out=m8[:], in0=m8[:], in1=trA[:, 0:8], op=ALU.add),
            deps=(ev_sq2,), sig=True)
        ev_a1 = p.op("vector", (lambda col: lambda v: v.scalar_tensor_tensor(
            out=trA[:, 0:8], in0=mods[:, col:col + 8], scalar=-1.0,
            in1=mods[:, col:col + 8], op0=ALU.mult, op1=ALU.max))(col),
            deps=(ev_g, ev_sgn), sig=True)
        ev_a2 = p.op("vector", lambda v: v.tensor_scalar(
            out=trA[:, 0:8], in0=trA[:, 0:8], scalar1=0.25, scalar2=None,
            op0=ALU.max), deps=(ev_a1,), sig=True)
        ev_g = p.op("vector", (lambda col: lambda v: v.tensor_tensor(
            out=mods[:, col:col + 8], in0=m8[:], in1=trA[:, 0:8],
            op=ALU.mult))(col), deps=(ev_a2, ev_sgn), sig=True)
    ev_mods_done = p.op("vector", lambda v: v.tensor_copy(out=m8[:], in_=mods[:, 0:8]),
                        deps=(ev_g, ev_ca1, ev_cb1), sig=True)

    prep_done = {}

    def prep_cast(src, dst, nrow128, ncols, first_dep=None):
        return p.op("gpsimd", lambda g: g.dma_start(out=dst, in_=src),
                    deps=(first_dep,), dma="g")

    def prep_quant(src, dst, nrow128, ncols, sw_col, first_dep=None):
        nsl = nrow128 * (ncols // 2048)
        ev_acc = None
        evd_hist = []
        red_hist = []
        i = 0
        for r in range(nrow128):
            for c0 in range(0, ncols, 2048):
                buf = prefA if i % 2 == 0 else prefB
                evd = p.op("gpsimd", (lambda r, c0, buf: lambda g: g.dma_start(
                    out=buf[:], in_=src[128 * r:128 * (r + 1), c0:c0 + 2048]))(r, c0, buf),
                    deps=([red_hist[-2]] if len(red_hist) >= 2 else list(first_dep or [])),
                    dma="g")
                ev_red = p.op("vector", (lambda i, buf: lambda v: v.tensor_reduce(
                    out=prep_r[:, i % 16:i % 16 + 1], in_=buf[:], axis=AX, op=ALU.add,
                    apply_absolute_value=True))(i, buf), deps=(evd,), sig=True)
                red_hist.append(ev_red)
                i += 1
                if i % 16 == 0 or i == nsl:
                    ncl = 16 if i % 16 == 0 else i % 16
                    first = i <= 16
                    tgt = 0 if first else 2
                    ev_f = p.op("vector", (lambda ncl, tgt: lambda v: v.tensor_scalar(
                        out=prep_r[:, 16:16 + ncl] if ncl <= 2 else prep_r[:, 0:ncl],
                        in0=prep_r[:, 0:ncl], scalar1=1.0, scalar2=None,
                        op0=ALU.mult, op1=ALU.add,
                        accum_out=trA[:, tgt:tgt + 1]))(ncl, tgt),
                        deps=(ev_red, ev_acc), sig=True)
                    if not first:
                        ev_f = p.op("vector", lambda v: v.tensor_tensor(
                            out=trA[:, 0:1], in0=trA[:, 0:1], in1=trA[:, 2:3], op=ALU.add),
                            deps=(ev_f,), sig=True)
                    ev_acc = ev_f
        ev_mm = p.op("tensor", lambda t: t.matmul(out=bcp[1][0:1, 0:1], lhsT=trA[:, 0:1],
                                                  rhs=onecol_f[:], start=True, stop=True),
                     deps=[ev_acc, ev_ones2] + bank_w("bc1"), sig=True)
        n = float(nrow128 * 128 * ncols)
        ev_r1 = p.op("vector", lambda v: v.tensor_scalar(
            out=rows[5][:, 0:1], in0=bcp[1][0:1, 0:1], scalar1=1.0 / n, scalar2=1e-5,
            op0=ALU.mult, op1=ALU.max), deps=(ev_mm,), sig=True)
        ev_is = p.op("vector", (lambda sw_col: lambda v: v.tensor_copy(
            out=srow[:, sw_col:sw_col + 1], in_=rows[5][:, 0:1]))(sw_col),
            deps=(ev_r1,), sig=True)
        ev_r2 = p.op("vector", lambda v: v.reciprocal(out=rows[5][:, 1:2], in_=rows[5][:, 0:1]),
                     deps=(ev_is,), sig=True)
        bank_r("bc1", ev_r1)
        ev_bc = p.op("tensor", lambda t: t.matmul(
            out=bcp[1][:, 4:5], lhsT=onesrow_f[:], rhs=rows[5][0:1, 1:2],
            start=True, stop=True), deps=[ev_r2, ev_onesrf] + bank_w("bc1"), sig=True)
        ev_swb = p.op("vector", (lambda sw_col: lambda v: v.tensor_copy(
            out=swb[:, sw_col:sw_col + 1], in_=bcp[1][:, 4:5]))(sw_col),
            deps=(ev_bc,), sig=True)
        bank_r("bc1", ev_swb)
        evo = None
        evq_hist = []
        i = 0
        for r in range(nrow128):
            for c0 in range(0, ncols, 2048):
                buf = prefA if i % 2 == 0 else prefB
                evd = p.op("gpsimd", (lambda r, c0, buf: lambda g: g.dma_start(
                    out=buf[:], in_=src[128 * r:128 * (r + 1), c0:c0 + 2048]))(r, c0, buf),
                    deps=(evq_hist[-2] if len(evq_hist) >= 2 else ev_swb,), dma="g")
                evi = p.op("vector", (lambda buf, sw_col: lambda v: v.tensor_scalar(
                    out=prep_i[:], in0=buf[:], scalar1=swb[:, sw_col:sw_col + 1],
                    scalar2=None, op0=ALU.mult))(buf, sw_col),
                    deps=(evd, ev_swb, evo), sig=True)
                evc = p.op("vector", lambda v: v.tensor_scalar(
                    out=prep_b[:], in0=prep_i[:], scalar1=1.0, scalar2=-1.0,
                    op0=ALU.min, op1=ALU.max), deps=(evi,), sig=True)
                evo = p.op("sync", (lambda r, c0: lambda s: s.dma_start(
                    out=dst[128 * r:128 * (r + 1), c0:c0 + 2048], in_=prep_b[:]))(r, c0),
                    deps=(evc,), dma="q")
                evq_hist.append(evo)
                i += 1
        return evo

    fl = lambda apx: apx.rearrange("(r c) m -> r (c m)", c=16)
    prep_done["q"] = prep_cast(dr["qT"], scr["q"], NCD, D)
    prep_done["k"] = prep_cast(dr["kT"], scr["k"], NCD, D)
    prep_done["v"] = prep_cast(dr["vT"], scr["v"], NCD, D)
    prep_done["o"] = prep_cast(dr["oT"], scr["o"], NCD, D)
    prep_done["fc"] = prep_cast(dr["fcB"], scr["fc"], NCI * D // 16 // 128, 2048)
    prep_done["proj"] = prep_cast(dr["projB"], scr["proj"],
                                  NCD * INNER // 16 // 128, 2048)
    prep_done["g"] = prep_quant(fl(dr["gB"]), fl(scr["g"]), NCI * D // 16 // 128, 2048, 0,
                                first_dep=[ev_mod_mm])
    prep_done["u"] = prep_quant(fl(dr["uB"]), fl(scr["u"]), NCI * D // 16 // 128, 2048, 1,
                                first_dep=[prep_done["g"]])
    prep_done["d"] = prep_quant(fl(dr["dB"]), fl(scr["d"]), NCD * INNER // 16 // 128, 2048, 2,
                                first_dep=[prep_done["u"]])

    wring = {"i": 0, "free": [None, None]}

    def wload(src_slice, ncols, kcn, extra_dep=None):
        i = wring["i"] % 2
        wring["i"] += 1
        ev = p.op("sync", (lambda i, src_slice, ncols, kcn: lambda s: s.dma_start(
            out=wbig[i][:, 0:ncols].rearrange("p (kc m) -> p kc m", kc=kcn),
            in_=src_slice))(i, src_slice, ncols, kcn),
            deps=(wring["free"][i], extra_dep), dma="q")
        return wbig[i], ev, i

    def wfree(slot, ev):
        wring["free"][slot] = ev

    x_src = dr["x_fm"].rearrange("(c p) t -> p c t", p=128)
    out_dst = out_fm.rearrange("(c p) t -> p c t", p=128)

    mmfree = [None, None, None, None]

    def pslot(i):
        return pP[i // 2][:, (i % 2) * 512:(i % 2) * 512 + 512]

    def emit_ln(src, dst, wcol_scale, wcol_bias, lnw, lnb, src_evs, extra_dst_deps=()):
        dep_all = list(src_evs)
        evc = p.op("vector", lambda v: v.tensor_copy(out=xbf[:], in_=src[:]),
                   deps=dep_all, sig=True)
        evs = p.op("vector", lambda v: v.scalar_tensor_tensor(
            out=x2bf[:], in0=src[:], scalar=1.0, in1=src[:], op0=ALU.mult, op1=ALU.mult),
            deps=dep_all, sig=True)
        ev_mm = None
        for c in range(NCD):
            ev_mm = p.op("tensor", (lambda c: lambda t: t.matmul(
                out=bcp[0][0:1, 0:chunk], lhsT=onescol_b[:],
                rhs=xbf[:, c * chunk:(c + 1) * chunk],
                start=(c == 0), stop=(c == NCD - 1)))(c),
                deps=[evc, ev_ones] + (bank_w("bc0") if c == 0 else []), sig=False)
        for c in range(NCD):
            ev_mm = p.op("tensor", (lambda c: lambda t: t.matmul(
                out=bcp[0][32:33, 0:chunk], lhsT=onescol_b[:],
                rhs=x2bf[:, c * chunk:(c + 1) * chunk],
                start=(c == 0), stop=(c == NCD - 1)))(c),
                deps=(evs,), sig=(c == NCD - 1))
        ev_mu = p.op("vector", lambda v: v.tensor_scalar(
            out=rows[0][:], in0=bcp[0][0:1, 0:chunk], scalar1=1.0 / D, scalar2=None,
            op0=ALU.mult), deps=(ev_mm,), sig=True)
        ev_m2 = p.op("vector", lambda v: v.tensor_tensor(
            out=rows[3][:], in0=rows[0][:], in1=rows[0][:], op=ALU.mult),
            deps=(ev_mu,), sig=True)
        ev_var = p.op("vector", lambda v: v.scalar_tensor_tensor(
            out=rows[4][:], in0=bcp[0][32:33, 0:chunk], scalar=1.0 / D,
            in1=rows[3][:], op0=ALU.mult, op1=ALU.subtract),
            deps=(ev_m2,), sig=True)
        bank_r("bc0", ev_var)
        ev_ln = p.op("scalar", lambda s: s.activation(
            out=rows[3][:], in_=rows[4][:], func=AF.Ln, bias=ceps[:]),
            deps=(ev_var, ev_cc), sig=True)
        ev_istd = p.op("scalar", lambda s: s.activation(
            out=rows[1][:], in_=rows[3][:], func=AF.Exp, scale=-0.5),
            deps=(ev_ln,), sig=True)
        ev_R = p.op("vector", lambda v: v.tensor_tensor(
            out=rows[2][:], in0=rows[0][:], in1=rows[1][:], op=ALU.mult),
            deps=(ev_istd,), sig=True)
        ev_bA = p.op("tensor", lambda t: t.matmul(
            out=bcp[0][:, 0:chunk], lhsT=onesrow_f[:], rhs=rows[1][:],
            start=True, stop=True), deps=[ev_istd, ev_mu, ev_onesrf] + bank_w("bc0"),
            sig=True)
        ev_bB = p.op("tensor", lambda t: t.matmul(
            out=bcp[1][:, 0:chunk], lhsT=onesrow_f[:], rhs=rows[2][:],
            start=True, stop=True), deps=[ev_R] + bank_w("bc1"), sig=True)
        evs_out = []
        for c in range(NCD):
            if lnw is None:
                scl = mods[:, wcol_scale + c:wcol_scale + c + 1]
                bia = mods[:, wcol_bias + c:wcol_bias + c + 1]
            else:
                scl = lnw[:, c:c + 1]
                bia = lnb[:, c:c + 1]
            vt = vtmp[c % 2]
            dep1 = [ev_bA, ev_mods_done] + dep_all
            if len(evs_out) >= 2:
                dep1.append(evs_out[-2])
            ev1 = p.op("vector", (lambda c, scl, vt: lambda v: v.scalar_tensor_tensor(
                out=vt[:], in0=src[:, c * chunk:(c + 1) * chunk], scalar=scl,
                in1=bcp[0][:, 0:chunk], op0=ALU.mult, op1=ALU.mult))(c, scl, vt),
                deps=dep1, sig=True)
            bank_r("bc0", ev1)
            ev2 = p.op("vector", (lambda c, scl, vt: lambda v: v.scalar_tensor_tensor(
                out=vt[:], in0=bcp[1][:, 0:chunk], scalar=scl, in1=vt[:],
                op0=ALU.mult, op1=ALU.subtract))(c, scl, vt),
                deps=(ev1, ev_bB), sig=True)
            bank_r("bc1", ev2)
            ev3 = p.op("scalar", (lambda c, bia, vt: lambda s: s.activation(
                out=dst[:, c * chunk:(c + 1) * chunk], in_=vt[:], func=AF.Identity,
                bias=bia, scale=-1.0))(c, bia, vt),
                deps=[ev2] + list(extra_dst_deps), sig=True)
            evs_out.append(ev3)
        return evs_out

    def part_max(src_ev):
        ev_t = None
        bw0 = bank_w("mid0")
        bw1 = bank_w("mid1")
        for b in range(4):
            ev_t = p.op("tensor", (lambda b: lambda t: t.transpose(
                out=midp[b % 2][:, 128 * (b // 2):128 * (b // 2) + 128],
                in_=trA[:, 128 * b:128 * (b + 1)], identity=cb["ident128f"][:]))(b),
                deps=[src_ev] + ev_const + (bw0 if b == 0 else (bw1 if b == 1 else [])),
                sig=(b >= 2))
        evs_r = []
        for b in range(4):
            evs_r.append(p.op("vector", (lambda b: lambda v: v.tensor_reduce(
                out=maxcol[:, b:b + 1], in_=midp[b % 2][:, 128 * (b // 2):128 * (b // 2) + 128],
                axis=AX, op=ALU.max))(b),
                deps=(ev_t,), sig=True))
        bank_r("mid0", evs_r[0]); bank_r("mid0", evs_r[2])
        bank_r("mid1", evs_r[1]); bank_r("mid1", evs_r[3])
        ev_f = None
        bwf = bank_w("bc1")
        for b in range(4):
            ev_f = p.op("tensor", (lambda b: lambda t: t.matmul(
                out=bcp[1][32:33, 128 * b:128 * (b + 1)], lhsT=maxcol[:, b:b + 1],
                rhs=cb["ident128f"][:], start=(b == 0), stop=(b == 3)))(b),
                deps=[evs_r[-1]] + (bwf if b == 0 else []), sig=(b == 3))
        ev_mr = p.op("vector", lambda v: v.tensor_scalar(
            out=rows[6][:], in0=bcp[1][32:33, 0:chunk], scalar1=1e-5, scalar2=None,
            op0=ALU.max), deps=(ev_f,), sig=True)
        bank_r("bc1", ev_mr)
        return ev_mr

    ev_store_prev = None
    ev_x_prev_readers = []
    ringfree = [None, None, None]

    for ci in range(nch):
        ev_x = p.op("sync", (lambda ci: lambda s: s.dma_start(
            out=xfm[:], in_=x_src[:, :, ci * chunk:(ci + 1) * chunk]))(ci),
            deps=[ev_store_prev] + ev_x_prev_readers, dma="q")
        ev_x_prev_readers = []

        evs_h = emit_ln(xfm, hfm, 8, 0, None, None,
                        [ev_x] + ([ev_mods_done] if ci == 0 else []))
        evs_xn = emit_ln(hfm, mmin, 0, 0, cb["ln1w_f"], cb["ln1b_f"], evs_h)
        if DBG and ci == 0:
            p.op("sync", lambda s: s.dma_start(out=dbg["dbg_mask"], in_=cb["mask128"][:]),
                 deps=ev_const, dma="q")
            p.op("sync", lambda s: s.dma_start(out=dbg["dbg_h"], in_=hfm[:]),
                 deps=evs_h + [("sV", p.counts["sV"]), ("sS", p.counts["sS"])], dma="q")
            p.op("sync", lambda s: s.dma_start(out=dbg["dbg_xn"], in_=mmin[:]),
                 deps=evs_xn, dma="q")
            p.op("sync", lambda s: s.dma_start(out=dbg["dbg_mods"], in_=mods[:]),
                 deps=[ev_mods_done], dma="q")

        if PHLE(1):
            ev_store_prev = p.op("sync", (lambda ci: lambda s: s.dma_start(
                out=out_dst[:, :, ci * chunk:(ci + 1) * chunk], in_=xfm[:]))(ci),
                deps=evs_xn, dma="q")
            ev_x_prev_readers = list(evs_xn)
            continue

        ev_amid_last = None
        ev_qkv_last = None
        for si in range(nslab):
            t0 = si * slab
            kh = {"q": khq, "k": khk, "v": khv}
            bias_b = {"q": qbb, "k": kbb, "v": vbb}
            ev_kh = {}
            for TN in ("q", "k", "v"):
                wsrc = scr[TN].rearrange("(kc pp) m -> pp kc m", pp=128)
                wb, evw, slot = wload(wsrc, 8 * D, 8, extra_dep=prep_done[TN])
                ev_evacs = []
                for hb in range(4):
                    bank = pP[hb % 2]
                    ev_mm = None
                    for hs in range(4):
                        h = hb * 4 + hs
                        for kc in range(NCD):
                            st = (hs % 2 == 0 and kc == 0)
                            ev_mm = p.op("tensor",
                                         (lambda h, hs, kc, wb, bank, t0, st: lambda t: t.matmul(
                                             out=bank[0:64, hs * slab:(hs + 1) * slab],
                                             lhsT=wb[:, kc * D + h * 64:kc * D + (h + 1) * 64],
                                             rhs=mmin[:, kc * chunk + t0:kc * chunk + t0 + slab],
                                             start=st, stop=False))(h, hs, kc, wb, bank, t0, st),
                                         deps=[evw, evs_xn[kc],
                                               mmfree[2 * (hb % 2)], mmfree[2 * (hb % 2) + 1]],
                                         sig=False)
                        ev_mm = p.op("tensor",
                                     (lambda h, hs, bank, TN: lambda t: t.matmul(
                                         out=bank[0:64, hs * slab:(hs + 1) * slab],
                                         lhsT=bias_b[TN][0:1, h * 64:(h + 1) * 64],
                                         rhs=onesrow_b[0:1, 0:slab],
                                         start=False, stop=(hs % 2 == 1)))(h, hs, bank, TN),
                                     deps=(ev_bb, ev_onesrb), sig=(hs == 3))
                    pin = bank[0:64, 0:4 * slab]
                    pin_v = pin.rearrange("p (hs t) -> p t hs", hs=4)
                    khdst = kh[TN][:, :].rearrange("p (t hh) -> p t hh", hh=H)[
                        :, :, 4 * hb:4 * hb + 4]
                    if TN == "v":
                        ev_e = p.op("vector",
                                    (lambda pin_v, khdst: lambda v: v.tensor_copy(
                                        out=khdst, in_=pin_v))(pin_v, khdst),
                                    deps=(ev_mm,), sig=True)
                    else:
                        er = Er[hb % 2]
                        ebr = EBr[hb % 2]
                        ev_a = p.op("scalar",
                                    (lambda pin, er: lambda s: s.activation(
                                        out=er, in_=pin, func=AF.Exp))(pin, er),
                                    deps=(ev_mm,), sig=True)
                        ev_b = p.op("vector",
                                    (lambda er, ebr: lambda v: v.tensor_scalar(
                                        out=ebr, in0=er,
                                        scalar1=1.0, scalar2=None, op0=ALU.min))(er, ebr),
                                    deps=(ev_a,), sig=True)
                        ebr_v = ebr.rearrange("p (hs t) -> p t hs", hs=4)
                        ev_e = p.op("vector",
                                    (lambda pin_v, ebr_v, khdst: lambda v: v.scalar_tensor_tensor(
                                        out=khdst, in0=pin_v, scalar=0.0, in1=ebr_v,
                                        op0=ALU.max, op1=ALU.add))(pin_v, ebr_v, khdst),
                                    deps=(ev_b,), sig=True)
                    ev_evacs.append(ev_e)
                    mmfree[2 * (hb % 2)] = ev_e
                    mmfree[2 * (hb % 2) + 1] = ev_e
                ev_kh[TN] = ev_evacs[-1]
                wfree(slot, ev_evacs[-1])
            dep_kh = [ev_kh["q"], ev_kh["k"], ev_kh["v"]]
            if DBG and ci == 0 and si == 0:
                p.op("sync", lambda s: s.dma_start(out=dbg["dbg_kq"], in_=khq[:]),
                     deps=dep_kh, dma="q")
                p.op("sync", lambda s: s.dma_start(out=dbg["dbg_kk"], in_=khk[:]),
                     deps=dep_kh, dma="q")
                p.op("sync", lambda s: s.dma_start(out=dbg["dbg_kv"], in_=khv[:]),
                     deps=dep_kh, dma="q")
            ev_qkv_last = ev_kh["v"]

            if PHLE(15):
                ev_amid_last = None
                continue

            grp_done = [None] * ngrp
            for g in range(ngrp):
                mname = "mid%d" % (g % 2)
                mb = midp[g % 2]
                vb = vst_ps[g % 2]
                sbg = sbd[g % 3]
                vt = vstr[g % 3]
                rb = recb[g % 3]
                ring_free = ringfree[g % 3]
                bw = bank_w(mname)
                ev_vt = p.op("tensor", (lambda g, vb: lambda t: t.transpose(
                    out=vb, in_=khv[:, 128 * g:128 * (g + 1)],
                    identity=cb["ident64"][:]))(g, vb),
                    deps=[*ev_const, ev_kh["v"]] + bw, sig=True)
                ev_vste = p.op("vector", (lambda vb, vt: lambda v: v.tensor_copy(
                    out=vt[:], in_=vb))(vb, vt), deps=(ev_vt, ring_free), sig=True)
                bank_r(mname, ev_vste)
                ev_S = p.op("tensor", (lambda g, mb: lambda t: t.matmul(
                    out=mb[:, 0:128], lhsT=khk[:, 128 * g:128 * (g + 1)],
                    rhs=khq[:, 128 * g:128 * (g + 1)], start=True, stop=True))(g, mb),
                    deps=dep_kh + [ev_vste], sig=True)
                ev_e = []
                for par in range(2):
                    s_in = mb[:, 0:128].rearrange("p (j hh pr) -> p hh j pr",
                                                  j=8, hh=8, pr=2)[:, :, :, par]
                    m_in = cb["mask128"][:, :].rearrange("p (j hh pr) -> p hh j pr",
                                                         j=8, hh=8, pr=2)[:, :, :, par]
                    s_out = sbg[:, 64 * par:64 * (par + 1)].rearrange(
                        "p (hh j) -> p hh j", hh=8)
                    ev_e.append(p.op("vector", (lambda s_in, m_in, s_out: lambda v:
                                     v.scalar_tensor_tensor(
                                         out=s_out, in0=s_in, scalar=1.0, in1=m_in,
                                         op0=ALU.mult, op1=ALU.mult))(s_in, m_in, s_out),
                                     deps=[ev_S, ring_free] + ev_const, sig=True))
                    bank_r(mname, ev_e[-1])
                if PHLE(21):
                    if DBG and ci == 0 and si == 0 and g == 0:
                        e1 = p.op("sync", (lambda sbg: lambda s: s.dma_start(
                            out=dbg["dbg_sbd"], in_=sbg[:]))(sbg), deps=ev_e, dma="q")
                        e2 = p.op("sync", (lambda vt: lambda s: s.dma_start(
                            out=dbg["dbg_vst"], in_=vt[:]))(vt), deps=[ev_vste], dma="q")
                        bank_r(mname, e1); bank_r(mname, e2)
                        ringfree[0] = e2
                    grp_done[g] = ev_e[-1] if PHASE == 21 else ev_vste
                    if not (DBG and ci == 0 and si == 0 and g == 0):
                        ringfree[g % 3] = grp_done[g]
                    continue
                ev_tapS = None
                if DBG and ci == 0 and si == 0 and g == 0:
                    ev_tapS = p.op("vector", (lambda mb: lambda v: v.tensor_copy(
                        out=trA[:, 0:128], in_=mb[:, 0:128]))(mb), deps=ev_e, sig=True)
                    p.op("sync", lambda s: s.dma_start(out=dbg["dbg_sp"], in_=trA[:, 0:128]),
                         deps=[ev_tapS], dma="q")
                ev_den = p.op("tensor", (lambda mb, sbg: lambda t: t.matmul(
                    out=mb[:, 256:257], lhsT=sbg[:], rhs=onescol_b[:],
                    start=True, stop=True))(mb, sbg),
                    deps=ev_e + [ev_ones, ev_tapS], sig=True)
                ev_rc = p.op("vector", (lambda g, mb: lambda v: v.reciprocal(
                    out=recd[:, g:g + 1], in_=mb[:, 256:257]))(g, mb),
                    deps=(ev_den,), sig=True)
                bank_r(mname, ev_rc)
                if PHLE(22):
                    if DBG and ci == 0 and si == 0 and g == 0:
                        e1 = p.op("sync", (lambda sbg: lambda s: s.dma_start(
                            out=dbg["dbg_sbd"], in_=sbg[:]))(sbg), deps=ev_e + [ev_rc], dma="q")
                        e3 = p.op("sync", lambda s: s.dma_start(
                            out=dbg["dbg_recd"], in_=recd[:]), deps=[ev_rc], dma="q")
                        bank_r(mname, e1); bank_r(mname, e3)
                        ringfree[0] = e3
                        grp_done[g] = e3
                    else:
                        grp_done[g] = ev_rc
                        ringfree[g % 3] = ev_rc
                    continue
                ev_n0 = p.op("tensor", (lambda mb, sbg, vt: lambda t: t.matmul(
                    out=mb[0:64, 128:192], lhsT=vt[:], rhs=sbg[:, 0:64],
                    start=True, stop=True))(mb, sbg, vt),
                    deps=[ev_vste, ev_rc] + ev_e, sig=(PHASE == 224))
                if PHLE(224):
                    grp_done[g] = ev_n0 if PHASE == 224 else ev_rc
                    ringfree[g % 3] = grp_done[g]
                    continue
                ev_n1 = p.op("tensor", (lambda mb, sbg, vt: lambda t: t.matmul(
                    out=mb[64:128, 128:192], lhsT=vt[:], rhs=sbg[:, 64:128],
                    start=True, stop=True))(mb, sbg, vt), deps=(), sig=(PHASE == 225))
                if PHLE(225):
                    if DBG and ci == 0 and si == 0 and g == 0:
                        e1 = p.op("sync", (lambda sbg: lambda s: s.dma_start(
                            out=dbg["dbg_sbd"], in_=sbg[:]))(sbg),
                            deps=ev_e + [ev_rc, ev_n1], dma="q")
                        ev_tapN = p.op("vector", (lambda mb: lambda v: v.tensor_copy(
                            out=trA[:, 128:192], in_=mb[:, 128:192]))(mb),
                            deps=[ev_n1, ev_rc], sig=True)
                        e2 = p.op("sync", lambda s: s.dma_start(
                            out=dbg["dbg_np"], in_=trA[:, 128:192]),
                            deps=[ev_tapN], dma="q")
                        bank_r(mname, e1); bank_r(mname, ev_tapN)
                        ringfree[0] = e2
                        grp_done[g] = e2
                    else:
                        grp_done[g] = ev_n1 if PHASE == 225 else ev_rc
                        ringfree[g % 3] = grp_done[g]
                    continue
                if PHASE == 226 and g > 0:
                    grp_done[g] = ev_n1
                    ringfree[g % 3] = ev_n1
                    continue
                ev_rb0 = p.op("tensor", (lambda g, mb: lambda t: t.matmul(
                    out=mb[0:64, 192:256],
                    lhsT=recd[:, g:g + 1].broadcast_to((128, 64)),
                    rhs=cb["ident128f"][:, 0:64], start=True, stop=True))(g, mb),
                    deps=(ev_rc,), sig=False)
                ev_rb1 = p.op("tensor", (lambda g, mb: lambda t: t.matmul(
                    out=mb[64:128, 192:256],
                    lhsT=recd[:, g:g + 1].broadcast_to((128, 64)),
                    rhs=cb["ident128f"][:, 64:128], start=True, stop=True))(g, mb),
                    deps=(), sig=True)
                if PHASE == 227 and g > 0:
                    grp_done[g] = ev_rb1
                    ringfree[g % 3] = ev_rb1
                    continue
                if PHASE == 228 and g > 0:
                    ev_rbe = p.op("vector", (lambda mb: lambda v: v.tensor_copy(
                        out=rbjunk[:], in_=mb[:, 192:256]))(mb), deps=(ev_rb1,), sig=True)
                elif PHASE == 229 and g > 0:
                    ev_rbe = p.op("vector", (lambda rb, sbg: lambda v: v.tensor_copy(
                        out=rb[:], in_=sbg[:, 0:64]))(rb, sbg), deps=(ev_rb1,), sig=True)
                else:
                    ev_rbe = p.op("vector", (lambda mb, rb: lambda v: v.tensor_copy(
                        out=rb[:], in_=mb[:, 192:256]))(mb, rb), deps=(ev_rb1,), sig=True)
                bank_r(mname, ev_rbe)
                if DBG and ci == 0 and si == 0 and g == 0:
                    ev_tapN = p.op("vector", (lambda mb: lambda v: v.tensor_copy(
                        out=trA[:, 128:192], in_=mb[:, 128:192]))(mb),
                        deps=[ev_rbe], sig=True)
                    p.op("sync", lambda s: s.dma_start(out=dbg["dbg_np"], in_=trA[:, 128:192]),
                         deps=[ev_tapN], dma="q")
                    bank_r(mname, ev_tapN)
                if PHASE in (226, 227, 228, 229, 231, 232):
                    if DBG and ci == 0 and si == 0 and g == 0:
                        e0 = p.op("sync", (lambda sbg: lambda s: s.dma_start(
                            out=dbg["dbg_vst"], in_=sbg[:, 0:64]))(sbg),
                            deps=ev_e, dma="q")
                        e1 = p.op("sync", (lambda sbg: lambda s: s.dma_start(
                            out=dbg["dbg_sbd"], in_=sbg[:]))(sbg),
                            deps=ev_e + [ev_rbe, e0], dma="q")
                        e3 = p.op("sync", lambda s: s.dma_start(
                            out=dbg["dbg_recd"], in_=recd[:]),
                            deps=[ev_rc, e1], dma="q")
                        e4 = p.op("sync", (lambda rb: lambda s: s.dma_start(
                            out=dbg["dbg_recb"], in_=rb[:]))(rb),
                            deps=[ev_rbe, e3], dma="q")
                        bank_r(mname, e1); bank_r(mname, e4)
                        ringfree[0] = e4
                        grp_done[g] = e4
                    else:
                        grp_done[g] = ev_rbe
                        ringfree[g % 3] = ev_rbe
                    continue
                dst = amid[:, :].rearrange("p (hh t) -> p hh t", hh=NCD)[
                    :, :, t0 + 8 * g:t0 + 8 * g + 8]
                ev_fin = p.op("vector", (lambda mb, rb, dst: lambda v: v.scalar_tensor_tensor(
                    out=dst, in0=mb[:, 128:192].rearrange("p (hh j) -> p hh j", hh=8),
                    scalar=1.0, in1=rb[:, :].rearrange("p (hh j) -> p hh j", hh=8),
                    op0=ALU.mult, op1=ALU.mult))(mb, rb, dst),
                    deps=(ev_rbe,), sig=True)
                bank_r(mname, ev_fin)
                ringfree[g % 3] = ev_fin
                grp_done[g] = ev_fin
                if DBG and ci == 0 and si == 0 and g == 0:
                    e1 = p.op("sync", (lambda sbg: lambda s: s.dma_start(
                        out=dbg["dbg_sbd"], in_=sbg[:]))(sbg), deps=ev_e, dma="q")
                    e2 = p.op("sync", (lambda vt: lambda s: s.dma_start(
                        out=dbg["dbg_vst"], in_=vt[:]))(vt), deps=[ev_vste], dma="q")
                    e3 = p.op("sync", lambda s: s.dma_start(
                        out=dbg["dbg_recd"], in_=recd[:]), deps=[ev_rc], dma="q")
                    e4 = p.op("sync", (lambda rb: lambda s: s.dma_start(
                        out=dbg["dbg_recb"], in_=rb[:]))(rb), deps=[ev_rbe], dma="q")
                    for _e in (e1, e2, e3, e4):
                        ringfree[g % 3] = _e
                        bank_r(mname, _e)
                    ringfree[0] = e4
            ev_amid_last = grp_done[-1]
            if DBG and PHASE in (232, 99) and ci == 0 and si == nslab - 1:
                _dump_srcs = [
                    (sbd[0][:, :], 0), (sbd[1][:, :], 128), (sbd[2][:, :], 256),
                    (recb[0][:, :].bitcast(BF16), 384),
                    (recb[1][:, :].bitcast(BF16), 512),
                    (recb[2][:, :].bitcast(BF16), 640),
                    (vstr[0][:, :], 768), (vstr[1][:, :], 832), (vstr[2][:, :], 896),
                    (recd[:, :].bitcast(BF16), 960),
                    (wbig[1][:, 8192 - 256:8192], 1024),
                    (wbig[0][:, 8192 - 256:8192], 1280),
                ]
                _prev = [d for d in grp_done if d is not None]
                for _src, _c0 in _dump_srcs:
                    _w = _src.shape[1]
                    _prev = [p.op("sync", (lambda _src, _c0, _w: lambda s: s.dma_start(
                        out=dbg["dbg_dump"][:, _c0:_c0 + _w], in_=_src))(_src, _c0, _w),
                        deps=_prev, dma="q")]
            if PHASE in (226, 227, 228, 229, 231, 232):
                ev_qkv_last = grp_done[-1]

        if PHLE(15) or PHLE(2):
            dps = [ev_amid_last, ev_qkv_last]
            ev_store_prev = p.op("sync", (lambda ci: lambda s: s.dma_start(
                out=out_dst[:, :, ci * chunk:(ci + 1) * chunk], in_=xfm[:]))(ci),
                deps=dps, dma="q")
            ev_x_prev_readers = [d for d in dps if d is not None]
            continue

        if DBG and ci == 0:
            p.op("sync", lambda s: s.dma_start(out=dbg["dbg_amid"], in_=amid[:]),
                 deps=[ev_amid_last], dma="q")

        wb, evw, slot = wload(scr["o"].rearrange("(kc pp) m -> pp kc m", pp=128),
                              8 * D, 8, extra_dep=prep_done["o"])
        ev_oev = []
        for oc in range(NCD):
            sl = oc % 4
            ev_mm = None
            for kc in range(NCD):
                ev_mm = p.op("tensor", (lambda oc, kc, wb, sl: lambda t: t.matmul(
                    out=pslot(sl), lhsT=wb[:, kc * D + oc * 128:kc * D + oc * 128 + 128],
                    rhs=amid[:, kc * chunk:(kc + 1) * chunk],
                    start=(kc == 0), stop=(kc == NCD - 1)))(oc, kc, wb, sl),
                    deps=[evw, mmfree[sl], ev_amid_last], sig=(kc == NCD - 1))
            ev_e = p.op("vector", (lambda oc, sl: lambda v: v.scalar_tensor_tensor(
                out=afm[:, oc * chunk:(oc + 1) * chunk], in0=pslot(sl),
                scalar=cb["ob_f"][:, oc:oc + 1],
                in1=hfm[:, oc * chunk:(oc + 1) * chunk],
                op0=ALU.add, op1=ALU.add))(oc, sl),
                deps=[ev_mm, evs_h[oc]] + ev_const, sig=True)
            mmfree[sl] = ev_e
            ev_oev.append(ev_e)
        wfree(slot, ev_oev[-1])

        if PHASE == 240:
            ev_store_prev = p.op("gpsimd", (lambda ci: lambda g_: g_.dma_start(
                out=out_dst[:, :, ci * chunk:(ci + 1) * chunk], in_=amid[:]))(ci),
                deps=[ev_amid_last], dma="g")
            ev_x_prev_readers = [ev_amid_last]
            continue
        if PHASE == 241:
            ev_store_prev = p.op("sync", (lambda ci: lambda s: s.dma_start(
                out=out_dst[:, :, ci * chunk:(ci + 1) * chunk], in_=afm[:]))(ci),
                deps=list(ev_oev), dma="q")
            ev_x_prev_readers = list(ev_oev)
            continue
        if DBG and ci == 0:
            p.op("sync", lambda s: s.dma_start(out=dbg["dbg_afm"], in_=afm[:]),
                 deps=ev_oev, dma="q")

        evs_xn2 = emit_ln(afm, mmin, 0, 0, cb["ln2w_f"], cb["ln2b_f"], ev_oev,
                          extra_dst_deps=[ev_qkv_last])

        ev_gelu = []
        for oc in range(NCI):
            wsrc = scr["fc"][oc * D:(oc + 1) * D, :].rearrange(
                "(kc pp) m -> pp kc m", pp=128)
            wb, evw, slot = wload(wsrc, D, 8, extra_dep=prep_done["fc"])
            sl = oc % 4
            ev_mm = None
            for kc in range(NCD):
                ev_mm = p.op("tensor", (lambda oc, kc, wb, sl: lambda t: t.matmul(
                    out=pslot(sl), lhsT=wb[:, kc * 128:(kc + 1) * 128],
                    rhs=mmin[:, kc * chunk:(kc + 1) * chunk],
                    start=(kc == 0), stop=(kc == NCD - 1)))(oc, kc, wb, sl),
                    deps=[evw, mmfree[sl], evs_xn2[kc]], sig=(kc == NCD - 1))
            ev_e = p.op("scalar", (lambda oc, sl: lambda s: s.activation(
                out=gfm[:, oc * chunk:(oc + 1) * chunk], in_=pslot(sl), func=AF.Gelu,
                bias=cb["fcb_f"][:, oc:oc + 1]))(oc, sl),
                deps=[ev_mm] + ev_const + ([prep_done["d"]] if ci == 0 else []), sig=True)
            mmfree[sl] = ev_e
            ev_gelu.append(ev_e)
            wfree(slot, ev_e)
        ev_projev = []
        for oc in range(NCD):
            wsrc = scr["proj"][oc * INNER:(oc + 1) * INNER, :].rearrange(
                "(kc pp) m -> pp kc m", pp=128)
            wb, evw, slot = wload(wsrc, 4 * D, 32, extra_dep=prep_done["proj"])
            sl = oc % 4
            ev_mm = None
            for kc in range(NCI):
                ev_mm = p.op("tensor", (lambda oc, kc, wb, sl: lambda t: t.matmul(
                    out=pslot(sl), lhsT=wb[:, kc * 128:(kc + 1) * 128],
                    rhs=gfm[:, kc * chunk:(kc + 1) * chunk],
                    start=(kc == 0), stop=(kc == NCI - 1)))(oc, kc, wb, sl),
                    deps=[evw, mmfree[sl], ev_gelu[kc]], sig=(kc == NCI - 1))
            vt = vtmp[oc % 2]
            dep_y = [ev_mm] + ev_const
            if len(ev_projev) >= 2:
                dep_y.append(ev_projev[-2])
            ev_y = p.op("vector", (lambda oc, sl, vt: lambda v: v.scalar_tensor_tensor(
                out=vt[:], in0=pslot(sl), scalar=cb["projb_f"][:, oc:oc + 1],
                in1=afm[:, oc * chunk:(oc + 1) * chunk],
                op0=ALU.add, op1=ALU.add))(oc, sl, vt),
                deps=dep_y, sig=True)
            ev_xu = p.op("vector", (lambda oc, vt: lambda v: v.scalar_tensor_tensor(
                out=xfm[:, oc * chunk:(oc + 1) * chunk], in0=vt[:],
                scalar=mods[:, 16 + oc:16 + oc + 1],
                in1=xfm[:, oc * chunk:(oc + 1) * chunk],
                op0=ALU.mult, op1=ALU.add))(oc, vt),
                deps=(ev_y, ev_mods_done), sig=True)
            mmfree[sl] = ev_xu
            ev_projev.append(ev_xu)
            wfree(slot, ev_y)

        if PHLE(3) or PHASE == 242:
            ev_store_prev = p.op("sync", (lambda ci: lambda s: s.dma_start(
                out=out_dst[:, :, ci * chunk:(ci + 1) * chunk], in_=xfm[:]))(ci),
                deps=ev_projev, dma="q")
            ev_x_prev_readers = list(ev_projev)
            continue

        if DBG and ci == 0:
            p.op("sync", lambda s: s.dma_start(out=dbg["dbg_gfm"], in_=gfm[:]),
                 deps=ev_gelu + [ev_projev[-1]], dma="q")
            p.op("sync", lambda s: s.dma_start(out=dbg["dbg_x1"], in_=xfm[:]),
                 deps=ev_projev, dma="q")

        evs_h2 = emit_ln(xfm, hfm, 32, 24, None, None, ev_projev,
                         extra_dst_deps=ev_oev)

        ev_trmax = None
        ev_trmin = None
        for c in range(1, NCD):
            if c == 1:
                ev_trmax = p.op("vector", lambda v: v.tensor_tensor(
                    out=trA[:], in0=hfm[:, 0:chunk], in1=hfm[:, chunk:2 * chunk],
                    op=ALU.max), deps=[evs_h2[0], evs_h2[1], evs_h2[-1]] +
                    ([("dq0", p.counts["dq0"]), ("dq1", p.counts["dq1"])] if DBG else []),
                    sig=True)
                ev_trmin = p.op("vector", lambda v: v.tensor_tensor(
                    out=vtmp[1][:], in0=hfm[:, 0:chunk], in1=hfm[:, chunk:2 * chunk],
                    op=ALU.min), deps=(evs_h2[0], evs_h2[1], evs_h2[-1], ev_trmax), sig=True)
            else:
                ev_trmax = p.op("vector", (lambda c: lambda v: v.tensor_tensor(
                    out=trA[:], in0=trA[:], in1=hfm[:, c * chunk:(c + 1) * chunk],
                    op=ALU.max))(c), deps=(ev_trmax, evs_h2[c]), sig=True)
                ev_trmin = p.op("vector", (lambda c: lambda v: v.tensor_tensor(
                    out=vtmp[1][:], in0=vtmp[1][:], in1=hfm[:, c * chunk:(c + 1) * chunk],
                    op=ALU.min))(c), deps=(ev_trmin, evs_h2[c]), sig=True)
        ev_tr = p.op("vector", lambda v: v.scalar_tensor_tensor(
            out=trA[:], in0=vtmp[1][:], scalar=-1.0, in1=trA[:],
            op0=ALU.mult, op1=ALU.max), deps=(ev_trmax, ev_trmin), sig=True)
        ev_mr = part_max(ev_tr)
        ev_l = p.op("scalar", lambda s: s.activation(out=rows[3][:], in_=rows[6][:],
                                                     func=AF.Ln), deps=(ev_mr,), sig=True)
        ev_sx = p.op("scalar", lambda s: s.activation(
            out=rows[3][:], in_=rows[3][:], func=AF.Exp, scale=-1.0,
            bias=cl127[:]), deps=(ev_l, ev_cc), sig=True)
        ev_sxb = p.op("tensor", lambda t: t.matmul(
            out=bcp[0][:, 0:chunk], lhsT=onesrow_f[:], rhs=rows[3][:],
            start=True, stop=True), deps=[ev_sx] + bank_w("bc0"), sig=True)
        ev_rg = p.op("vector", lambda v: v.tensor_scalar(
            out=rows[5][:], in0=rows[6][:], scalar1=srow[:, 0:1], scalar2=1.0 / 127.0,
            op0=ALU.mult, op1=ALU.mult), deps=(ev_mr, prep_done["g"]), sig=True)
        ev_rgb = p.op("tensor", lambda t: t.matmul(
            out=bcp[1][:, 0:chunk], lhsT=onesrow_f[:], rhs=rows[5][:],
            start=True, stop=True), deps=[ev_rg] + bank_w("bc1"), sig=True)
        ev_rgt = p.op("scalar", lambda s: s.activation(
            out=rsxg_t[:], in_=bcp[1][:, 0:chunk], func=AF.Identity),
            deps=(ev_rgb,), sig=True)
        bank_r("bc1", ev_rgt)
        ev_keep = p.op("vector", lambda v: v.tensor_copy(out=rows[4][:], in_=rows[6][:]),
                       deps=(ev_mr,), sig=True)
        evs_xq = []
        for c in range(NCD):
            dq = [ev_sxb, evs_h2[c], ev_gelu[-1]]
            if evs_xq:
                dq.append(evs_xq[-1])
            ev_q1 = p.op("vector", (lambda c: lambda v: v.tensor_tensor(
                out=qi[:], in0=hfm[:, c * chunk:(c + 1) * chunk], in1=bcp[0][:, 0:chunk],
                op=ALU.mult))(c), deps=dq, sig=True)
            bank_r("bc0", ev_q1)
            ev_q2 = p.op("vector", (lambda c: lambda v: v.tensor_copy(
                out=mmin[:, c * chunk:(c + 1) * chunk], in_=qi[:]))(c),
                deps=(ev_q1,), sig=True)
            evs_xq.append(ev_q2)

        if PHASE == 243:
            ev_store_prev = p.op("sync", (lambda ci: lambda s: s.dma_start(
                out=out_dst[:, :, ci * chunk:(ci + 1) * chunk], in_=hfm[:]))(ci),
                deps=evs_h2 + [ev_trmin, ev_tr], dma="q")
            ev_x_prev_readers = evs_h2
            continue
        if PHASE == 244:
            ev_store_prev = p.op("gpsimd", (lambda ci: lambda g_: g_.dma_start(
                out=out_dst[:, :, ci * chunk:(ci + 1) * chunk], in_=mmin[:]))(ci),
                deps=list(evs_xq), dma="g")
            ev_x_prev_readers = list(evs_xq)
            continue
        if DBG and ci == 0:
            p.op("sync", lambda s: s.dma_start(out=dbg["dbg_h2"], in_=hfm[:]),
                 deps=evs_h2 + [ev_trmin], dma="q")
            p.op("sync", lambda s: s.dma_start(out=dbg["dbg_xq"], in_=mmin[:]),
                 deps=evs_xq, dma="q")

        ev_tr2 = None
        ev_prod = []
        for oc in range(NCI):
            wg_src = scr["g"][oc * D:(oc + 1) * D, :].rearrange("(kc pp) m -> pp kc m", pp=128)
            wu_src = scr["u"][oc * D:(oc + 1) * D, :].rearrange("(kc pp) m -> pp kc m", pp=128)
            wbg_, evwg, slotg = wload(wg_src, D, 8, extra_dep=prep_done["g"])
            wbu_, evwu, slotu = wload(wu_src, D, 8, extra_dep=prep_done["u"])
            slg = oc % 2
            slu = 2 + oc % 2
            ev_mg = None
            ev_muu = None
            for kc in range(NCD):
                ev_mg = p.op("tensor", (lambda oc, kc, wbg_, slg: lambda t: t.matmul(
                    out=pslot(slg), lhsT=wbg_[:, kc * 128:(kc + 1) * 128],
                    rhs=mmin[:, kc * chunk:(kc + 1) * chunk],
                    start=(kc == 0), stop=(kc == NCD - 1)))(oc, kc, wbg_, slg),
                    deps=[evwg, mmfree[slg], evs_xq[kc]], sig=(kc == NCD - 1))
            for kc in range(NCD):
                ev_muu = p.op("tensor", (lambda oc, kc, wbu_, slu: lambda t: t.matmul(
                    out=pslot(slu), lhsT=wbu_[:, kc * 128:(kc + 1) * 128],
                    rhs=mmin[:, kc * chunk:(kc + 1) * chunk],
                    start=(kc == 0), stop=(kc == NCD - 1)))(oc, kc, wbu_, slu),
                    deps=[evwu, mmfree[slu], evs_xq[kc]], sig=(kc == NCD - 1))
            gt = gtr[oc % 2]
            sg = sigr[oc % 2]
            gu = gur[oc % 2]
            dep_p1 = [ev_mg, ev_rgt]
            if len(ev_prod) >= 2:
                dep_p1.append(ev_prod[-2])
            ev_p1 = p.op("vector", (lambda slg, gt: lambda v: v.tensor_tensor(
                out=gt[:], in0=pslot(slg), in1=rsxg_t[:], op=ALU.mult))(slg, gt),
                deps=dep_p1, sig=True)
            ev_p2 = p.op("scalar", (lambda gt, sg: lambda s: s.activation(
                out=sg[:], in_=gt[:], func=AF.Sigmoid))(gt, sg), deps=(ev_p1,), sig=True)
            ev_p3 = p.op("vector", (lambda slu, gt, gu: lambda v: v.tensor_tensor(
                out=gu[:], in0=pslot(slu), in1=gt[:], op=ALU.mult))(slu, gt, gu),
                deps=(ev_muu, ev_p1), sig=True)
            dep_p4 = [ev_p2, ev_p3]
            if oc == 0:
                dep_p4.append(ev_projev[-1])
            ev_p4 = p.op("vector", (lambda oc, gu, sg: lambda v: v.tensor_tensor(
                out=gfm[:, oc * chunk:(oc + 1) * chunk], in0=gu[:], in1=sg[:],
                op=ALU.mult))(oc, gu, sg), deps=dep_p4, sig=True)
            mmfree[slg] = ev_p3
            mmfree[slu] = ev_p3
            wfree(slotg, ev_mg)
            wfree(slotu, ev_muu)
            ev_prod.append(ev_p4)
            if oc == 1:
                ev_tr2 = p.op("vector", lambda v: v.tensor_tensor(
                    out=trAb[:], in0=gfm[:, 0:chunk], in1=gfm[:, chunk:2 * chunk],
                    op=ALU.max), deps=(ev_prod[0], ev_p4, ev_mr), sig=True)
                ev_tr2n = p.op("vector", lambda v: v.tensor_tensor(
                    out=trBb[:], in0=gfm[:, 0:chunk], in1=gfm[:, chunk:2 * chunk],
                    op=ALU.min), deps=(ev_prod[0], ev_p4, ev_tr2), sig=True)
            elif oc > 1:
                ev_tr2 = p.op("vector", (lambda oc: lambda v: v.tensor_tensor(
                    out=trAb[:], in0=trAb[:], in1=gfm[:, oc * chunk:(oc + 1) * chunk],
                    op=ALU.max))(oc), deps=(ev_tr2, ev_p4), sig=True)
                ev_tr2n = p.op("vector", (lambda oc: lambda v: v.tensor_tensor(
                    out=trBb[:], in0=trBb[:], in1=gfm[:, oc * chunk:(oc + 1) * chunk],
                    op=ALU.min))(oc), deps=(ev_tr2n, ev_p4), sig=True)

        ev_tr2c = p.op("vector", lambda v: v.scalar_tensor_tensor(
            out=trA[:], in0=trBb[:], scalar=-1.0, in1=trAb[:],
            op0=ALU.mult, op1=ALU.max), deps=(ev_tr2, ev_tr2n), sig=True)
        ev_mr2 = part_max(ev_tr2c)
        ev_l2 = p.op("scalar", lambda s: s.activation(out=rows[3][:], in_=rows[6][:],
                                                      func=AF.Ln), deps=(ev_mr2,), sig=True)
        ev_sx2 = p.op("scalar", lambda s: s.activation(
            out=rows[3][:], in_=rows[3][:], func=AF.Exp, scale=-1.0,
            bias=cl127[:]), deps=(ev_l2, ev_cc), sig=True)
        ev_sxb2 = p.op("tensor", lambda t: t.matmul(
            out=bcp[0][:, 0:chunk], lhsT=onesrow_f[:], rhs=rows[3][:],
            start=True, stop=True), deps=[ev_sx2] + bank_w("bc0"), sig=True)
        ev_sxt2 = p.op("scalar", lambda s: s.activation(
            out=sxb2_t[:], in_=bcp[0][:, 0:chunk], func=AF.Identity),
            deps=(ev_sxb2,), sig=True)
        bank_r("bc0", ev_sxt2)
        ev_sc1 = p.op("vector", lambda v: v.tensor_tensor(
            out=rows[5][:], in0=rows[4][:], in1=rows[6][:], op=ALU.mult),
            deps=(ev_keep, ev_mr2), sig=True)
        ev_sc2 = p.op("vector", lambda v: v.tensor_scalar(
            out=rows[5][:], in0=rows[5][:], scalar1=srow[:, 1:2],
            scalar2=1.0 / (127.0 * 127.0), op0=ALU.mult, op1=ALU.mult),
            deps=(ev_sc1, prep_done["u"]), sig=True)
        ev_sc3 = p.op("vector", lambda v: v.tensor_scalar(
            out=rows[5][:], in0=rows[5][:], scalar1=srow[:, 2:3], scalar2=None,
            op0=ALU.mult), deps=(ev_sc2, prep_done["d"]), sig=True)
        ev_scb = p.op("tensor", lambda t: t.matmul(
            out=bcp[1][:, 0:chunk], lhsT=onesrow_f[:], rhs=rows[5][:],
            start=True, stop=True), deps=[ev_sc3] + bank_w("bc1"), sig=True)
        ev_sct = p.op("scalar", lambda s: s.activation(
            out=scaleb_t[:], in_=bcp[1][:, 0:chunk], func=AF.Identity),
            deps=(ev_scb,), sig=True)
        bank_r("bc1", ev_sct)

        evs_pq = []
        for oc in range(NCI):
            dq = [ev_sxt2, ev_prod[oc], ev_tr2]
            if evs_pq:
                dq.append(evs_pq[-1])
            ev_q1 = p.op("vector", (lambda oc: lambda v: v.tensor_tensor(
                out=qi[:], in0=gfm[:, oc * chunk:(oc + 1) * chunk], in1=sxb2_t[:],
                op=ALU.mult))(oc), deps=dq, sig=True)
            ev_q2 = p.op("vector", (lambda oc: lambda v: v.tensor_copy(
                out=gfm[:, oc * chunk:(oc + 1) * chunk], in_=qi[:]))(oc),
                deps=(ev_q1,), sig=True)
            evs_pq.append(ev_q2)

        if PHLE(4):
            ev_store_prev = p.op("sync", (lambda ci: lambda s: s.dma_start(
                out=out_dst[:, :, ci * chunk:(ci + 1) * chunk], in_=xfm[:]))(ci),
                deps=[evs_pq[-1]], dma="q")
            ev_x_prev_readers = [evs_pq[-1]]
            continue

        if PHASE == 245:
            ev_store_prev = p.op("gpsimd", (lambda ci: lambda g_: g_.dma_start(
                out=out_dst[:, :, ci * chunk:(ci + 1) * chunk],
                in_=gfm[:, 0:NCD * chunk]))(ci),
                deps=list(evs_pq), dma="g")
            ev_x_prev_readers = list(evs_pq)
            continue
        if DBG and ci == 0:
            p.op("sync", lambda s: s.dma_start(out=dbg["dbg_pq"], in_=gfm[:]),
                 deps=evs_pq, dma="q")

        if PHASE == 247:
            ev_store_prev = p.op("gpsimd", (lambda: lambda g_: g_.dma_start(
                out=out_fm[0:1024, 0:128],
                in_=scr["d"][SCRD_OFF:SCRD_OFF + 1024, 0:128]))(),
                deps=[prep_done["d"], evs_pq[-1]], dma="g")
            ev_x_prev_readers = [evs_pq[-1]]
            continue
        if PHASE == 246:
            ev_store_prev = p.op("sync", (lambda ci: lambda s: s.dma_start(
                out=out_dst[:, 0:1, ci * chunk:(ci + 1) * chunk],
                in_=scaleb_t[:].rearrange("p (c t) -> p c t", c=1)))(ci),
                deps=[ev_sct] + list(evs_pq), dma="q")
            ev_x_prev_readers = [ev_sct]
            continue

        ev_downev = []
        ev_st250 = []
        ev_yh = []
        for oc in range(1 if PHASE == 248 else NCD):
            wsrc = scr["d"][oc * INNER:(oc + 1) * INNER, :].rearrange(
                "(kc pp) m -> pp kc m", pp=128)
            wb, evw, slot = wload(wsrc, 4 * D, 32, extra_dep=prep_done["d"])
            sl = oc % 4
            ev_mm = None
            for kc in range(NCI):
                ev_mm = p.op("tensor", (lambda oc, kc, wb, sl: lambda t: t.matmul(
                    out=pslot(sl), lhsT=wb[:, kc * 128:(kc + 1) * 128],
                    rhs=gfm[:, kc * chunk:(kc + 1) * chunk],
                    start=(kc == 0), stop=(kc == NCI - 1)))(oc, kc, wb, sl),
                    deps=[evw, mmfree[sl], evs_pq[kc]], sig=(kc == NCI - 1))
            vt = vtmp[oc % 2]
            dep_y = [ev_mm, ev_sct, ev_mods_done]
            if len(ev_downev) >= 2:
                dep_y.append(ev_downev[-2])
            if PHASE in (250, 251) and len(ev_st250) >= 2:
                dep_y.append(ev_st250[-2])
            if PHASE == 252 and ev_downev:
                dep_y.append(ev_downev[-1])
            if PHASE == 254:
                if len(ev_yh) >= 2:
                    dep_y.append(ev_yh[-2])
                ev_yc = p.op("scalar", (lambda sl, vt: lambda s: s.activation(
                    out=vt[:], in_=pslot(sl), func=AF.Identity))(sl, vt),
                    deps=dep_y, sig=True)
                ev_y = p.op("vector", (lambda oc, vt: lambda v: v.scalar_tensor_tensor(
                    out=trA[:], in0=vt[:], scalar=mods[:, 40 + oc:40 + oc + 1],
                    in1=scaleb_t[:], op0=ALU.mult, op1=ALU.mult))(oc, vt),
                    deps=(ev_yc,), sig=True)
                ev_yh.append(ev_y)
                ev_xu = p.op("vector", (lambda oc: lambda v: v.tensor_tensor(
                    out=xfm[:, oc * chunk:(oc + 1) * chunk],
                    in0=xfm[:, oc * chunk:(oc + 1) * chunk], in1=trA[:],
                    op=ALU.add))(oc), deps=(ev_y,), sig=True)
                mmfree[sl] = ev_xu
                ev_downev.append(ev_xu)
                wfree(slot, ev_yc)
                continue
            ev_y = p.op("vector", (lambda oc, sl, vt: lambda v: v.scalar_tensor_tensor(
                out=vt[:], in0=pslot(sl), scalar=mods[:, 40 + oc:40 + oc + 1],
                in1=scaleb_t[:], op0=ALU.mult, op1=ALU.mult))(oc, sl, vt),
                deps=dep_y, sig=True)
            if PHASE in (250, 251):
                ev_st250.append(p.op("sync", (lambda ci, oc, vt: lambda s: s.dma_start(
                    out=out_dst[:, oc:oc + 1, ci * chunk:(ci + 1) * chunk],
                    in_=vt[:].rearrange("p (c t) -> p c t", c=1)))(ci, oc, vt),
                    deps=[ev_y], dma="q"))
            ev_xu = p.op("vector", (lambda oc, vt: lambda v: v.tensor_tensor(
                out=xfm[:, oc * chunk:(oc + 1) * chunk],
                in0=xfm[:, oc * chunk:(oc + 1) * chunk], in1=vt[:], op=ALU.add))(oc, vt),
                deps=(ev_y,), sig=True)
            mmfree[sl] = ev_xu
            ev_downev.append(ev_xu)
            wfree(slot, ev_y)
            if PHASE == 248 and oc == 0:
                ev_store_248 = p.op("sync", (lambda ci, vt: lambda s: s.dma_start(
                    out=out_dst[:, 0:1, ci * chunk:(ci + 1) * chunk],
                    in_=vt[:].rearrange("p (c t) -> p c t", c=1)))(ci, vt),
                    deps=[ev_y, ev_xu], dma="q")

        if PHASE == 248:
            ev_store_prev = ev_store_248
            ev_x_prev_readers = list(ev_downev)
            continue
        if PHASE == 250:
            ev_store_prev = ev_st250[-1]
            ev_x_prev_readers = list(ev_downev)
            continue
        if PHASE == 251:
            ev_store_prev = p.op("sync", (lambda ci: lambda s: s.dma_start(
                out=out_dst[:, :, ci * chunk:(ci + 1) * chunk], in_=xfm[:]))(ci),
                deps=ev_downev + [ev_st250[-1], ev_st250[-2]], dma="q")
            ev_x_prev_readers = list(ev_downev)
            continue
        ev_store_prev = p.op("sync", (lambda ci: lambda s: s.dma_start(
            out=out_dst[:, :, ci * chunk:(ci + 1) * chunk], in_=xfm[:]))(ci),
            deps=ev_downev, dma="q")
        ev_x_prev_readers = list(ev_downev)

    p.op("sync", lambda s: None, deps=[ev_store_prev])

    with nc.Block() as block:
        def emitter(eng_name):
            def run(eng):
                waited = {}
                for emit, deps, incs in p.ops[eng_name]:
                    for (s, v) in deps:
                        if waited.get(s, 0) < v:
                            eng.wait_ge(sems[s], v)
                            waited[s] = v
                    inst = emit(eng)
                    if inst is not None:
                        for (s, amt) in incs:
                            inst.then_inc(sems[s], amt)
            return run

        block.sync(emitter("sync"))
        block.tensor(emitter("tensor"))
        block.vector(emitter("vector"))
        block.scalar(emitter("scalar"))
        block.gpsimd(emitter("gpsimd"))

    ctx.close()
    return nc



def _host_inputs(inputs, b, ttot):
    f32 = np.float32
    x = np.ascontiguousarray(np.asarray(inputs["L0_attn_x"][b], dtype=f32)[:ttot, :].T)

    def T(w):
        return np.ascontiguousarray(np.asarray(w, dtype=f32).T)

    def blocked(wT, nb):
        i_, o_ = wT.shape
        return np.ascontiguousarray(
            wT.reshape(i_, nb, 128).transpose(1, 0, 2).reshape(nb * i_, 128))

    qT = T(inputs["L0_attn_q_w"]); kT = T(inputs["L0_attn_k_w"])
    vT = T(inputs["L0_attn_v_w"]); oT = T(inputs["L0_attn_o_w"])
    fcT = T(inputs["L0_attn_fc_w"])
    projT = T(inputs["L0_attn_proj_w"])
    gT = T(inputs["L0_ffn_w_gate"]); uT = T(inputs["L0_ffn_w_up"])
    dT = T(inputs["L0_ffn_w_down"])

    fm = lambda vec, ncol: np.ascontiguousarray(
        np.asarray(vec, dtype=f32).reshape(ncol, 128).T)

    mask = np.zeros((128, 128), dtype=ml_dtypes.bfloat16)
    for j in range(8):
        mask[16 * j:16 * (j + 1), 16 * j:16 * (j + 1)] = 1
    ident64 = np.eye(64, dtype=ml_dtypes.bfloat16)
    ident128f = np.eye(128, dtype=f32)

    return {
        "x_fm": x,
        "qT": qT, "kT": kT, "vT": vT, "oT": oT,
        "fcB": blocked(fcT, NCI), "projB": blocked(projT, NCD),
        "gB": blocked(gT, NCI), "uB": blocked(uT, NCI), "dB": blocked(dT, NCD),
        "wmod": np.ascontiguousarray(np.asarray(inputs["L0_wmod"], dtype=f32)),
        "cfm": fm(np.asarray(inputs["L0_c"], dtype=f32)[b], NCD),
        "qb_r": np.ascontiguousarray(np.asarray(inputs["L0_attn_q_b"], dtype=f32).reshape(1, D)),
        "kb_r": np.ascontiguousarray(np.asarray(inputs["L0_attn_k_b"], dtype=f32).reshape(1, D)),
        "vb_r": np.ascontiguousarray(np.asarray(inputs["L0_attn_v_b"], dtype=f32).reshape(1, D)),
        "ob_f": fm(inputs["L0_attn_o_b"], NCD),
        "projb_f": fm(inputs["L0_attn_proj_b"], NCD),
        "fcb_f": fm(inputs["L0_attn_fc_b"], NCI),
        "ln1w_f": fm(inputs["L0_attn_ln1_w"], NCD),
        "ln1b_f": fm(inputs["L0_attn_ln1_b"], NCD),
        "ln2w_f": fm(inputs["L0_attn_ln2_w"], NCD),
        "ln2b_f": fm(inputs["L0_attn_ln2_b"], NCD),
        "mask128": mask, "ident64": ident64, "ident128f": ident128f,
    }


def kernel(**inputs):
    ttot = int(inputs["L0_attn_x"].shape[1])
    ncores = int(inputs["L0_attn_x"].shape[0])
    nc = build_nc(ttot=ttot)
    in_maps = [_host_inputs(inputs, b, ttot) for b in range(ncores)]
    res = run_bass_kernel_spmd(nc, in_maps, core_ids=list(range(ncores)))
    outs = []
    for b in range(ncores):
        ofm = np.asarray(res.results[b]["out_fm"])
        outs.append(np.ascontiguousarray(ofm.T))
    return np.stack(outs, axis=0).astype(np.float32)
